# version 01
# baseline (speedup 1.0000x reference)
import numpy as np
import concourse.bass as bass
import concourse.mybir as mybir
from concourse.bass_utils import run_bass_kernel_spmd
from contextlib import ExitStack

F32 = mybir.dt.float32
BF16 = mybir.dt.bfloat16
I32 = mybir.dt.int32
AF = mybir.ActivationFunctionType
ALU = mybir.AluOpType
AX = mybir.AxisListType.X

NCORES = 8
BL = 2
L = 2048
D = 1024
TOK = BL * L
NT = TOK // 128
NTB = L // 128
DCH = D // 128
E = 2048
ECH = E // 128
NST = 64
EPS = 1e-5

ENG2BLOCK = {"pe": "tensor", "act": "scalar", "ve": "vector", "gp": "gpsimd", "sp": "sync"}

_UNIQ = [0]


def _sbu(nc, st, name, shape, dtype):
    _UNIQ[0] += 1
    return st.enter_context(nc.sbuf_tensor(f"{name}_u{_UNIQ[0]}", shape, dtype))

_PHASES = 4
_TAP = None
_DBG = None
_SERDMA = False


class Prog:

    def __init__(self, nc):
        self.nc = nc
        self.ops = []
        self.bufs = {}
        self.tiny = set()
        self.pending_barrier = None
        self.post_barrier_seen = set()

    def _buf(self, key):
        return self.bufs.setdefault(key, {"w": None, "r": []})

    def op(self, eng, fn, reads=(), writes=(), hard=(), deps=(), inc=1, raw_inc=False):
        o = {"i": len(self.ops), "eng": eng, "fn": fn, "deps": [], "hard": set(),
             "inc": inc, "need_inc": False, "cum": None, "raw_inc": raw_inc}
        dd = list(deps)
        hardset = set(hard) | {k for k in reads if k in self.tiny}
        for key in list(reads) + list(hard):
            st = self._buf(key)
            if st["w"] is not None:
                dd.append(st["w"])
                if key in hardset:
                    o["hard"].add(st["w"]["i"])
        for key in writes:
            st = self._buf(key)
            if st["w"] is not None:
                dd.append(st["w"])
            dd.extend(st["r"])
        if self.pending_barrier is not None and eng not in self.post_barrier_seen:
            dd.extend(self.pending_barrier)
            self.post_barrier_seen.add(eng)
        seen = {}
        for d in dd:
            if d is not o:
                seen[d["i"]] = d
        o["deps"] = list(seen.values())
        for key in list(reads) + list(hard):
            self._buf(key)["r"].append(o)
        for key in writes:
            st = self._buf(key)
            st["w"] = o
            st["r"] = []
        self.ops.append(o)
        return o

    def dma(self, eng, out_ap, in_ap, reads=(), writes=(), deps=(), slow=False):
        def fn(e, out_ap=out_ap, in_ap=in_ap, slow=slow):
            return e.dma_start(out=out_ap, in_=in_ap, allow_slow_non_contiguous=slow)
        o = self.op(eng, fn, reads=reads, writes=writes, deps=deps, inc=16)
        o["need_inc"] = True
        return o

    def barrier(self):
        last = {}
        for o in self.ops:
            last[o["eng"]] = o
        self.pending_barrier = list(last.values())
        self.post_barrier_seen = set()

    DMA_SLOTS = 24

    def emit(self, stack):
        nc = self.nc
        for o in self.ops:
            for d in o["deps"]:
                d["need_inc"] = True
                if d["eng"] != o["eng"] or d["i"] in o["hard"]:
                    d["xdep"] = True
        sems = {eng: stack.enter_context(nc.semaphore(f"sem_{eng}")) for eng in ENG2BLOCK}
        dsems = {}
        for eng in ("sp", "gp"):
            dsems[eng] = [stack.enter_context(nc.semaphore(f"dma_{eng}{i}"))
                          for i in range(self.DMA_SLOTS)]
        nc.all_engine_barrier()
        allsems = list(sems.values()) + [s for ss in dsems.values() for s in ss]
        nums = sorted(s.num for s in allsems)
        assert nums == list(range(nums[0], nums[0] + len(nums)))
        nc.gpsimd.dma_reset(range(nums[0], nums[-1] + 1))
        nc.gpsimd.sem_clear(range(nums[0], nums[-1] + 1))
        nc.all_engine_barrier()
        cnt = dict.fromkeys(ENG2BLOCK, 0)
        dcnt = {eng: [0] * self.DMA_SLOTS for eng in dsems}
        didx = dict.fromkeys(dsems, 0)
        for o in self.ops:
            if o["inc"] == 16:
                slot = didx[o["eng"]] % self.DMA_SLOTS
                didx[o["eng"]] += 1
                o["slot"] = slot
                o["issue_wait"] = dcnt[o["eng"]][slot] * 16
                dcnt[o["eng"]][slot] += 1
                o["cum"] = dcnt[o["eng"]][slot] * 16
                o["semobj"] = dsems[o["eng"]][slot]
                o["semkey"] = (o["eng"], slot)
            elif o["need_inc"]:
                cnt[o["eng"]] += o["inc"]
                o["cum"] = cnt[o["eng"]]
                o["semobj"] = sems[o["eng"]]
                o["semkey"] = o["eng"]
        byeng = {k: [] for k in ENG2BLOCK}
        for o in self.ops:
            byeng[o["eng"]].append(o)
        block = stack.enter_context(nc.Block())

        def make_body(eng, ops):
            def body(e):
                hwm = {}
                for o in ops:
                    if o["inc"] == 16 and o["issue_wait"] > 0:
                        k = o["semkey"]
                        if o["issue_wait"] > hwm.get(k, 0):
                            e.wait_ge(o["semobj"], o["issue_wait"])
                            hwm[k] = o["issue_wait"]
                    for d in o["deps"]:
                        if d["eng"] == eng and d["i"] not in o["hard"] and d["inc"] != 16:
                            continue
                        if d["cum"] is None:
                            continue
                        k = d["semkey"]
                        if d["cum"] > hwm.get(k, 0):
                            e.wait_ge(d["semobj"], d["cum"])
                            hwm[k] = d["cum"]
                    inst = o["fn"](e)
                    if o["inc"] == 16:
                        inst.then_inc(o["semobj"], 16)
                    elif o["need_inc"]:
                        if o.get("xdep") and not o["raw_inc"] and eng in ("ve", "act", "gp"):
                            e.drain().then_inc(o["semobj"], o["inc"])
                        else:
                            inst.then_inc(o["semobj"], o["inc"])
            return body

        for eng, bname in ENG2BLOCK.items():
            getattr(block, bname)(make_body(eng, byeng[eng]))


def alloc_tiny(nc, st, prefixes, p=None):
    tiny = {}
    for pref in prefixes:
        for suf in ("_sums", "_sumsq", "_mu", "_msq", "_var", "_ln", "_rstd", "_nmr"):
            tiny[pref + suf] = _sbu(nc, st, pref + suf, [128, 1], F32)
            if p is not None:
                p.tiny.add(pref + suf)
    return tiny


def ln_stats(p, src_ap, src_key, scratch_ap, scratch_key, tiny, pref, n_free):
    k = pref
    sums, sumsq = tiny[k + "_sums"], tiny[k + "_sumsq"]
    t_mu, t_msq, t_var, t_ln = tiny[k + "_mu"], tiny[k + "_msq"], tiny[k + "_var"], tiny[k + "_ln"]
    rstd, nmr = tiny[k + "_rstd"], tiny[k + "_nmr"]
    p.op("ve", lambda e: e.reduce_sum(sums[:], src_ap, axis=AX),
         reads=[src_key], writes=[k + "_sums"])
    p.op("act", lambda e: e.activation(scratch_ap, src_ap, AF.Square, accum_out=sumsq[:]),
         reads=[src_key], writes=[scratch_key, k + "_sumsq"])
    p.op("ve", lambda e: e.tensor_scalar(t_mu[:], sums[:], 1.0 / n_free, None, ALU.mult),
         reads=[k + "_sums"], writes=[k + "_mu"])
    p.op("ve", lambda e: e.tensor_mul(t_msq[:], t_mu[:], t_mu[:]),
         reads=[k + "_mu"], writes=[k + "_msq"])
    p.op("ve", lambda e: e.scalar_tensor_tensor(t_var[:], sumsq[:], 1.0 / n_free, t_msq[:],
                                                ALU.mult, ALU.subtract),
         reads=[k + "_sumsq", k + "_msq"], writes=[k + "_var"])
    p.op("act", lambda e: e.activation(t_ln[:], t_var[:], AF.Ln, bias=float(EPS)),
         reads=[k + "_var"], hard=["eps_t"], writes=[k + "_ln"])
    p.op("act", lambda e: e.activation(rstd[:], t_ln[:], AF.Exp, scale=-0.5),
         reads=[k + "_ln"], writes=[k + "_rstd"])
    p.op("ve", lambda e: e.tensor_scalar(nmr[:], t_mu[:], rstd[:], -1.0, ALU.mult, ALU.mult),
         reads=[k + "_mu"], hard=[k + "_rstd"], writes=[k + "_nmr"])
    return rstd, nmr, (k + "_rstd", k + "_nmr")


def ln_apply(p, out_ap, out_key, src_ap, src_key, rstd, nmr, hard_keys):
    p.op("act", lambda e: e.activation(out_ap, src_ap, AF.Identity, scale=rstd[:], bias=nmr[:]),
         reads=[src_key], hard=list(hard_keys), writes=[out_key])


def dma_bcast_scalar(p, dst, dst_name, src_ap, src_key):
    p.dma("sp", dst[0:1, 0:1], src_ap, reads=[src_key], writes=[(dst_name, 1)])
    sz = 1
    while sz < 128:
        s2 = min(sz, 128 - sz)
        p.dma("sp", dst[sz:sz + s2, 0:1], dst[0:s2, 0:1],
              reads=[(dst_name, sz)], writes=[(dst_name, sz + s2)])
        sz += s2
    return (dst_name, 128)


_NC_CACHE = {}


def build():
    nc = bass.Bass("TRN2", target_bir_lowering=False)
    p = Prog(nc)

    def din(name, shape, dtype=F32):
        return nc.dram_tensor(name, shape, dtype, kind="ExternalInput").ap()

    x_in = din("x_in", [TOK, D])
    Wq = din("Wq", [D, D]); WqT = din("WqT", [D, D]); Wk = din("Wk", [D, D]); Wv = din("Wv", [D, D])
    Wqa = din("Wqa", [D, D]); Wka = din("Wka", [D, D])
    bq = din("bq", [1, D]); bk = din("bk", [1, D]); bv = din("bv", [1, D])
    bqa = din("bqa", [1, D]); bka = din("bka", [1, D])
    g0 = din("g0", [1, D]); be0c = din("be0c", [D, 1])
    wffT = din("wffT", [D, D]); gff = din("gff", [1, D]); bff = din("bff", [1, D])
    ln1w = din("ln1w", [1, D]); ln1b = din("ln1b", [1, D])
    ln2w = din("ln2w", [1, D]); ln2b = din("ln2b", [1, D])
    mW = {"WinT": [], "convw": [], "convb": [], "WxpT": [], "WdtT": [], "dtb": [],
          "Alog": [], "Dp": [], "WoutT": []}
    for mi in (1, 2):
        mW["WinT"].append(din(f"m{mi}_WinT", [D, 2 * E]))
        mW["convw"].append(din(f"m{mi}_convw", [E, 4]))
        mW["convb"].append(din(f"m{mi}_convb", [E, 1]))
        mW["WxpT"].append(din(f"m{mi}_WxpT", [E, 192]))
        mW["WdtT"].append(din(f"m{mi}_WdtT", [NST, E]))
        mW["dtb"].append(din(f"m{mi}_dtb", [E, 1]))
        mW["Alog"].append(din(f"m{mi}_Alog", [E, NST]))
        mW["Dp"].append(din(f"m{mi}_Dp", [E, 1]))
        mW["WoutT"].append(din(f"m{mi}_WoutT", [E, D]))
    WrT = din("WrT", [D, D]); WkkT = din("WkkT", [D, D]); WvvT = din("WvvT", [D, D])
    murc = din("murc", [D, 1]); mukc = din("mukc", [D, 1])

    out_hbm = nc.dram_tensor("out", [TOK, D], F32, kind="ExternalOutput").ap()

    def scr(name, shape, dtype=F32):
        return nc.dram_tensor(name, shape, dtype, kind="Internal").ap()

    dbg_h = scr("dbg_h", [TOK, D])
    x1_h = scr("x1_h", [TOK, D])
    xn1_h = scr("xn1_h", [TOK, D])
    x2_h = scr("x2_h", [TOK, D])
    xn_h = scr("xn_h", [TOK, D])
    xnT_h = scr("xnT_h", [BL, D, L], BF16)
    xc_h = scr("xc_h", [E, L])
    z_h = scr("z_h", [E, L])
    ybf_h = scr("ybf_h", [E, L], BF16)
    xcb_h = scr("xcb_h", [E, L], BF16)
    xm_h = scr("xm_h", [D, L])
    x3_h = scr("x3_h", [TOK, D])
    xn3T_h = scr("xn3T_h", [BL, D, L])
    cc_in_h = nc.dram_tensor("cc_in_h", [1, 1], F32).ap()
    cc_out_h = nc.dram_tensor("cc_out_h", [1, 1], F32).ap()

    scratches = {"x1_h": x1_h, "xn1_h": xn1_h, "x2_h": x2_h, "xn_h": xn_h, "x3_h": x3_h,
                 "dbg_h": dbg_h}

    with ExitStack() as stack:
        ident_f = stack.enter_context(nc.sbuf_tensor("ident_f", [128, 128], F32))
        ident_b = stack.enter_context(nc.sbuf_tensor("ident_b", [128, 128], BF16))
        tri_b = stack.enter_context(nc.sbuf_tensor("tri_b", [128, 128], BF16))
        ones_f = stack.enter_context(nc.sbuf_tensor("ones_f", [128, 128], F32))
        ones_row_b = stack.enter_context(nc.sbuf_tensor("ones_row_b", [1, 128], BF16))
        tri_f = stack.enter_context(nc.sbuf_tensor("tri_f", [128, 128], F32))
        invt = stack.enter_context(nc.sbuf_tensor("invt", [128, NTB], F32))
        A_sb = stack.enter_context(nc.sbuf_tensor("A_sb", [128, 2 * ECH * NST], F32))
        gmax_bc = stack.enter_context(nc.sbuf_tensor("gmax_bc", [128, 1], F32))
        for k in ("qscale_s", "qscale_i", "gm_sc", "gm_row", "wsum", "wacc", "wsc0",
                  "wsc_d", "wsc_ma", "wsc_m", "wsc_r", "wsc_alpha", "wsc_as",
                  "dtb", "dp", "convw", "convb", "d_mur", "d_muk"):
            p.tiny.add(k)
        qscale = stack.enter_context(nc.sbuf_tensor("qscale", [1, 2], F32))
        absrun = stack.enter_context(nc.sbuf_tensor("absrun", [128, 1], F32))
        alpha_s_bc = stack.enter_context(nc.sbuf_tensor("alpha_s_bc", [128, 1], F32))
        nopb = stack.enter_context(nc.sbuf_tensor("nopb", [1, 2], F32))
        eps_t = stack.enter_context(nc.sbuf_tensor("eps_t", [128, 1], F32))
        one_t = stack.enter_context(nc.sbuf_tensor("one_t", [128, 1], F32))
        bc = {}
        for nm in ("ln2w", "ln2b"):
            bc[nm] = stack.enter_context(nc.sbuf_tensor("bc_" + nm, [128, D], F32))
        stAB = ExitStack()
        for nm in ("g0", "gff", "bff", "ln1w", "ln1b"):
            bc[nm] = stAB.enter_context(nc.sbuf_tensor("bc_" + nm, [128, D], F32))

        nc.const_aps.register(F32, float(EPS), eps_t[:, 0:1])
        nc.const_aps.register(F32, 1.0, one_t[:, 0:1])
        p.op("gp", lambda e: e.memset(eps_t[:], float(EPS)), writes=["eps_t"])
        p.op("gp", lambda e: e.memset(one_t[:], 1.0), writes=["one_t"])
        p.op("gp", lambda e: e.memset(ones_f[:], 1.0), writes=["ones_f"])
        p.op("gp", lambda e: e.memset(ones_row_b[:], 1.0), writes=["ones_row_b"])
        p.op("gp", lambda e: e.affine_select(ident_f[:], ones_f[:], [[1, 128]], ALU.is_equal, 0.0,
                                             base=0, channel_multiplier=-1),
             reads=["ones_f"], writes=["ident_f"])
        p.op("ve", lambda e: e.tensor_copy(ident_b[:], ident_f[:]), reads=["ident_f"], writes=["ident_b"])
        p.op("gp", lambda e: e.affine_select(tri_f[:], ones_f[:], [[1, 128]], ALU.is_ge, 0.0,
                                             base=0, channel_multiplier=-1),
             reads=["ones_f"], writes=["tri_f"])
        p.op("ve", lambda e: e.tensor_copy(tri_b[:], tri_f[:]), reads=["tri_f"], writes=["tri_b"])
        p.op("gp", lambda e: e.iota(invt[:], [[128, NTB]], base=1, channel_multiplier=1,
                                    allow_small_or_imprecise_dtypes=True), writes=["invt"])
        p.op("ve", lambda e: e.reciprocal(invt[:], invt[:]), reads=["invt"], writes=["invt"])
        p.op("gp", lambda e: e.memset(absrun[:], 0.0), writes=["absrun"])

        st0 = ExitStack()
        row_f = st0.enter_context(nc.sbuf_tensor("row_f", [1, D], F32))
        ps_bc = st0.enter_context(nc.psum_tensor("ps_bc", [128, 512], F32))
        for nm, src in (("g0", g0), ("gff", gff), ("bff", bff), ("ln1w", ln1w),
                        ("ln1b", ln1b), ("ln2w", ln2w), ("ln2b", ln2b)):
            p.dma("sp", row_f[:], src, writes=["row_f"])
            for c in range(2):
                p.op("pe", lambda e, c=c: e.matmul(ps_bc[:, :], ones_f[0:1, :],
                                                   row_f[0:1, c * 512:(c + 1) * 512],
                                                   start=True, stop=True),
                     reads=["ones_f", "row_f"], writes=["ps_bc"])
                p.op("act", lambda e, nm=nm, c=c: e.activation(bc[nm][:, c * 512:(c + 1) * 512],
                                                               ps_bc[:, :], AF.Identity),
                     reads=["ps_bc"], writes=["bc_" + nm + "_p"])
            p.op("ve", lambda e: e.memset(nopb[:], 0.0), reads=["bc_" + nm + "_p"], writes=["bc_" + nm, "nopb"])
        a_tmp = st0.enter_context(nc.sbuf_tensor("a_tmp", [128, NST], F32))
        for mi in range(2):
            for ec in range(ECH):
                cbase = (mi * ECH + ec) * NST
                p.dma("sp", a_tmp[:], mW["Alog"][mi][ec * 128:(ec + 1) * 128, :], writes=["a_tmp"])
                p.op("act", lambda e, cbase=cbase: e.activation(
                    A_sb[:, cbase:cbase + NST], a_tmp[:], AF.Exp),
                    reads=["a_tmp"], writes=[("A", cbase)])
                p.op("ve", lambda e, cbase=cbase: e.tensor_scalar(
                    A_sb[:, cbase:cbase + NST], A_sb[:, cbase:cbase + NST], -1.0, None, ALU.mult),
                    reads=[("A", cbase)], writes=[("A", cbase)])

        st0.close()
        stA = ExitStack()
        build_phase_a(nc, p, stA, dict(Wq=Wq, WqT=WqT, Wk=Wk, Wv=Wv, Wqa=Wqa, Wka=Wka,
                                       bq=bq, bk=bk, bv=bv, bqa=bqa, bka=bka, be0c=be0c),
                      x_in, x1_h, xn1_h, bc, ident_b, ident_f, tri_b, ones_row_b, ones_f, invt, absrun, dbg_h)
        gm_row = stA.enter_context(nc.sbuf_tensor("gm_row", [1, 128], F32))
        gm_sc = stA.enter_context(nc.sbuf_tensor("gm_sc", [1, 1], F32))
        if _DBG == "absrun":
            p.dma("sp", dbg_h[0:128, 0:1], absrun[:, 0:1], reads=["absrun"],
                  writes=[("dbg_h", 0)], slow=True)
        p.dma("sp", gm_row[0:1, :], absrun[:, 0:1],
              reads=["absrun"], writes=["gm_row"], slow=True)
        if _DBG == "gmrow":
            p.dma("sp", dbg_h[0:1, 0:128], gm_row[0:1, :], reads=["gm_row"], writes=[("dbg_h", 0)])
        p.op("ve", lambda e: e.reduce_max(gm_sc[:], gm_row[:], axis=AX), reads=["gm_row"], writes=["gm_sc"])
        if _DBG == "gmsc":
            p.dma("sp", dbg_h[0:1, 0:1], gm_sc[:], reads=["gm_sc"], writes=[("dbg_h", 0)])
        p.dma("gp", cc_in_h, gm_sc[:], reads=["gm_sc"], writes=["cc_in_h"])
        p.tiny.add("cc_out_h")
        p.op("gp", lambda e: e.collective_compute(
            "AllReduce", ALU.max, replica_groups=[list(range(NCORES))],
            ins=[cc_in_h], outs=[cc_out_h]),
            reads=["cc_in_h"], writes=["cc_out_h"], raw_inc=True)
        p.dma("gp", gm_sc[:], cc_out_h, reads=["cc_out_h"], writes=["gm_sc"])
        p.op("ve", lambda e: e.tensor_scalar(qscale[0:1, 0:1], gm_sc[:], 1e-8, 1.0 / 127.0,
                                             ALU.max, ALU.mult),
             reads=["gm_sc"], writes=["qscale_s"])
        p.op("ve", lambda e: e.reciprocal(qscale[0:1, 1:2], qscale[0:1, 0:1]),
             reads=["qscale_s"], writes=["qscale_i"])
        gb_last = dma_bcast_scalar(p, gmax_bc, "gb", qscale[0:1, 1:2], "qscale_i")
        p.barrier()
        stA.close()

        if _PHASES >= 2:
            stB = ExitStack()
            build_phase_b(nc, p, stB, wffT, xn1_h, x1_h, x2_h, xn_h, xnT_h, bc,
                          ident_b, gmax_bc, gb_last, qscale, alpha_s_bc, dbg_h)
            p.barrier()
            stB.close()

        stAB.close()
        if _PHASES >= 3:
            build_phase_c(nc, p, stack, mW, A_sb, ident_b, ident_f, tri_f, bc,
                          xnT_h, xn_h, x2_h, xc_h, z_h, ybf_h, xcb_h, xm_h, x3_h, xn3T_h)

        if _PHASES >= 4 and _TAP is None:
            build_phase_d(nc, p, stack, WrT, WkkT, WvvT, murc, mukc,
                          xn3T_h, x3_h, z_h, xm_h, out_hbm, ident_f)
        else:
            tap = scratches[_TAP] if _TAP else x1_h
            tname = _TAP if _TAP else "x1_h"
            stT = ExitStack()
            tbuf = stT.enter_context(nc.sbuf_tensor("tap_t", [128, D], F32))
            for i in range(NT):
                p.dma("sp", tbuf[:], tap[i * 128:(i + 1) * 128, :],
                      reads=[(tname, i)], writes=["tap_t"])
                p.dma("sp", out_hbm[i * 128:(i + 1) * 128, :], tbuf[:],
                      reads=["tap_t"], writes=[("out_h", i)])
            stT.close()

        outdeps = [p.bufs[k]["w"] for k in p.bufs
                   if isinstance(k, tuple) and k[0] == "out_h" and p.bufs[k]["w"] is not None]
        p.op("sp", lambda e: e.nop(), deps=outdeps)
        p.emit(stack)
    return nc


def build_phase_a(nc, p, stA, W, x_in, x1_h, xn1_h, bc, ident_b, ident_f, tri_b, ones_row_b,
                  ones_f, invt, absrun, dbg_h):
    w_sb = {}
    for nm in ("Wq", "Wk", "Wv", "Wka", "Wqa2"):
        w_sb[nm] = stA.enter_context(nc.sbuf_tensor("w_" + nm, [128, DCH * D], BF16))
    wld = [stA.enter_context(nc.sbuf_tensor(f"wldA{i}", [128, D], F32)) for i in range(2)]
    for nm in ("Wq", "Wk", "Wv", "Wka"):
        for kc in range(DCH):
            sl = kc % 2
            p.dma("sp", wld[sl][:], W[nm][kc * 128:(kc + 1) * 128, :], writes=[("wldA", sl)])
            p.op("act", lambda e, nm=nm, kc=kc, sl=sl: e.activation(
                w_sb[nm][:, kc * D:(kc + 1) * D], wld[sl][:], AF.Copy),
                reads=[("wldA", sl)], writes=[("w", nm, kc)])
    brow = {nm: stA.enter_context(nc.sbuf_tensor("brow_" + nm, [1, D], BF16))
            for nm in ("q", "k", "v", "sq", "ka")}
    be0_b = stA.enter_context(nc.sbuf_tensor("be0_b", [128, DCH], BF16))
    be0_f = stA.enter_context(nc.sbuf_tensor("be0_f", [128, DCH], F32))
    brow_f = stA.enter_context(nc.sbuf_tensor("brow_f", [1, D], F32))
    bpq_c = stA.enter_context(nc.sbuf_tensor("bpq_c", [128, DCH], BF16))
    stW = ExitStack()
    wqt_b = stW.enter_context(nc.sbuf_tensor("wqt_b", [128, DCH * D], BF16))
    wqa_b = stW.enter_context(nc.sbuf_tensor("wqa_b", [128, DCH * D], BF16))
    for nm, dst in (("WqT", wqt_b), ("Wqa", wqa_b)):
        for kc in range(DCH):
            sl = kc % 2
            p.dma("sp", wld[sl][:], W[nm][kc * 128:(kc + 1) * 128, :], writes=[("wldA", sl)])
            p.op("act", lambda e, dst=dst, kc=kc, sl=sl: e.activation(
                dst[:, kc * D:(kc + 1) * D], wld[sl][:], AF.Copy),
                reads=[("wldA", sl)], writes=[(nm, kc)])
    ps_q = stA.enter_context(nc.psum_tensor("ps_q", [128, 1024], F32))
    ps_s = stA.enter_context(nc.psum_tensor("ps_s", [128, 1024], F32))
    ps_t = stA.enter_context(nc.psum_tensor("ps_t", [128, 1024], BF16))
    ps_c = stA.enter_context(nc.psum_tensor("ps_c", [128, 1024], F32))
    for dc in range(DCH):
        for c in range(2):
            for kc in range(DCH):
                p.op("pe", lambda e, dc=dc, c=c, kc=kc: e.matmul(
                    ps_q[:, c * 512:(c + 1) * 512],
                    wqt_b[:, kc * D + dc * 128: kc * D + (dc + 1) * 128],
                    wqa_b[:, kc * D + c * 512:kc * D + (c + 1) * 512],
                    start=(kc == 0), stop=(kc == DCH - 1)),
                    reads=[("WqT", kc), ("Wqa", kc)], writes=["ps_q"])
        p.op("act", lambda e, dc=dc: e.activation(w_sb["Wqa2"][:, dc * D:(dc + 1) * D],
                                                  ps_q[:, :], AF.Copy),
             reads=["ps_q"], writes=[("w", "Wqa2", dc)])

    p.dma("sp", be0_f[:], W["be0c"].rearrange("(c p) one -> p (c one)", p=128), writes=["be0_f"], slow=True)
    p.op("ve", lambda e: e.tensor_copy(be0_b[:], be0_f[:]), reads=["be0_f"], writes=["be0_b"])

    def bias_row(nm, wname, bsrc):
        p.dma("sp", brow_f[:], bsrc, writes=["brow_f"])
        p.op("ve", lambda e, nm=nm: e.tensor_copy(brow[nm][:], brow_f[:]),
             reads=["brow_f"], writes=[("brow0", nm)])
        for c in range(2):
            p.op("pe", lambda e, nm=nm, c=c: e.matmul(
                ps_s[0:1, c * 512:(c + 1) * 512], ones_row_b[0:1, 0:1],
                brow[nm][0:1, c * 512:(c + 1) * 512], start=True, stop=False),
                reads=[("brow0", nm), "ones_row_b"], writes=["ps_s"])
            for kc in range(DCH):
                p.op("pe", lambda e, wname=wname, kc=kc, c=c: e.matmul(
                    ps_s[0:1, c * 512:(c + 1) * 512], be0_b[:, kc:kc + 1],
                    w_sb[wname][:, kc * D + c * 512: kc * D + (c + 1) * 512],
                    start=False, stop=(kc == DCH - 1)),
                    reads=["be0_b", ("w", wname, kc)], writes=["ps_s"])
        p.op("act", lambda e, nm=nm: e.activation(brow[nm][:], ps_s[0:1, :], AF.Copy),
             reads=["ps_s"], writes=[("brow", nm)])

    bias_row("q", "Wq", W["bq"])
    bias_row("k", "Wk", W["bk"])
    bias_row("v", "Wv", W["bv"])
    bpq_h = nc.dram_tensor("bpq_h", [1, D], BF16).ap()
    p.dma("sp", bpq_h, brow["q"][0:1, :], reads=[("brow", "q")], writes=["bpq_h"])
    p.dma("sp", bpq_c[:], bpq_h.rearrange("one (c p) -> p (one c)", p=128),
          reads=["bpq_h"], writes=["bpq_c"], slow=True)
    p.dma("sp", brow_f[:], W["bqa"], writes=["brow_f"])
    p.op("ve", lambda e: e.tensor_copy(brow["sq"][:], brow_f[:]), reads=["brow_f"], writes=[("brow0", "sq")])
    for c in range(2):
        p.op("pe", lambda e, c=c: e.matmul(
            ps_s[0:1, c * 512:(c + 1) * 512], ones_row_b[0:1, 0:1],
            brow["sq"][0:1, c * 512:(c + 1) * 512], start=True, stop=False),
            reads=[("brow0", "sq"), "ones_row_b"], writes=["ps_s"])
        for kc in range(DCH):
            p.op("pe", lambda e, kc=kc, c=c: e.matmul(
                ps_s[0:1, c * 512:(c + 1) * 512], bpq_c[:, kc:kc + 1],
                wqa_b[:, kc * D + c * 512: kc * D + (c + 1) * 512],
                start=False, stop=(kc == DCH - 1)),
                reads=["bpq_c", ("Wqa", kc)], writes=["ps_s"])
    p.op("act", lambda e: e.activation(brow["sq"][:], ps_s[0:1, :], AF.Copy),
         reads=["ps_s"], writes=[("brow", "sq")])
    p.dma("sp", brow_f[:], W["bka"], writes=["brow_f"])
    p.op("ve", lambda e: e.tensor_copy(brow["ka"][:], brow_f[:]), reads=["brow_f"], writes=[("brow", "ka")])
    p.barrier()
    stW.close()

    tiny = alloc_tiny(nc, stA, ["a0", "a1", "a2", "a3", "a4"], p)
    sm = {nm: stA.enter_context(nc.sbuf_tensor("sm_" + nm, [128, 1], F32))
          for nm in ("nmx", "nmxs", "esum", "rs", "nkx", "nkxs", "ksum", "rk", "am")}
    for nm in sm:
        p.tiny.add("sm_" + nm)
    for k in ("carry_q", "carry_k", "carry_f", "absrun"):
        p.tiny.add(k)
    T = {}
    for nm in ("scrt", "ln0", "hgt", "qkf", "vf", "qwkw", "cqck", "pqpk", "mixedf"):
        T[nm] = stA.enter_context(nc.sbuf_tensor(nm, [128, D], F32))
    for nm in ("hg", "hgT", "ppre", "kpre", "mixT"):
        T[nm] = stA.enter_context(nc.sbuf_tensor(nm, [128, D], BF16))
    xt = [stA.enter_context(nc.sbuf_tensor(f"xt{i}", [128, D], F32)) for i in range(2)]
    carry_q = stA.enter_context(nc.sbuf_tensor("carry_q", [1, D], BF16))
    carry_k = stA.enter_context(nc.sbuf_tensor("carry_k", [1, D], BF16))
    carry_f = stA.enter_context(nc.sbuf_tensor("carry_f", [1, D], F32))

    def mm_chain(psname, ps_ap, lhsT, lhsT_key, w_name, bias_name):
        for c in range(2):
            p.op("pe", lambda e, c=c, bias_name=bias_name: e.matmul(
                ps_ap[:, c * 512:(c + 1) * 512], ones_row_b[0:1, :],
                brow[bias_name][0:1, c * 512:(c + 1) * 512], start=True, stop=False),
                reads=[("brow", bias_name), "ones_row_b"], writes=[psname])
            for kc in range(DCH):
                p.op("pe", lambda e, c=c, kc=kc, w_name=w_name, lhsT=lhsT: e.matmul(
                    ps_ap[:, c * 512:(c + 1) * 512], lhsT[:, kc * 128:(kc + 1) * 128],
                    w_sb[w_name][:, kc * D + c * 512:kc * D + (c + 1) * 512],
                    start=False, stop=(kc == DCH - 1)),
                    reads=[lhsT_key, ("w", w_name, kc)], writes=[psname])

    def transpose8(src, src_key, dst, dst_key, ident=None, ident_key="ident_b", ps=None, ps_key="ps_t"):
        idt = ident if ident is not None else ident_b
        pst = ps if ps is not None else ps_t
        for j in range(DCH):
            p.op("pe", lambda e, j=j, idt=idt, pst=pst: e.transpose(
                pst[:, j * 128:(j + 1) * 128], src[:, j * 128:(j + 1) * 128], idt[:]),
                 reads=[src_key, ident_key], writes=[ps_key])
        p.op("act", lambda e, pst=pst: e.activation(dst[:], pst[:, 0:1024], AF.Copy),
             reads=[ps_key], writes=[dst_key])

    def softmax(psname, ps_ap, outbuf, outkey, pn, px, ps_, pr):
        p.op("ve", lambda e: e.reduce_max(sm[pn][:], ps_ap[:, :], axis=AX, negate=True),
             reads=[psname], writes=["sm_" + pn])
        p.op("act", lambda e: e.activation(sm[px][:], sm[pn][:], AF.Copy, scale=1.0 / 32.0),
             reads=["sm_" + pn], writes=["sm_" + px])
        p.op("act", lambda e: e.activation(outbuf[:], ps_ap[:, :], AF.Exp, scale=1.0 / 32.0,
                                           bias=sm[px][:], accum_out=sm[ps_][:]),
             reads=[psname], hard=["sm_" + px], writes=[outkey, "sm_" + ps_])
        p.op("ve", lambda e: e.reciprocal(sm[pr][:], sm[ps_][:]),
             reads=["sm_" + ps_], writes=["sm_" + pr])

    for i in range(NT):
        j = i % NTB
        xti = xt[i % 2]
        xkey = ("xt", i % 2)

        def dbg(nm, ap, key, i=i):
            if _DBG == nm:
                p.dma("sp", dbg_h[i * 128:(i + 1) * 128, :], ap, reads=[key],
                      writes=[("dbg_h", i)])

        def dbgc(nm, ap, key, i=i):
            if _DBG == nm:
                p.dma("sp", dbg_h[i * 128:(i + 1) * 128, 0:1], ap, reads=[key],
                      writes=[("dbg_h", i)], slow=True)

        p.dma("sp", xti[:], x_in[i * 128:(i + 1) * 128, :], writes=[xkey])
        r0, n0, h0 = ln_stats(p, xti[:], xkey, T["scrt"][:], "scrt", tiny, "a0", D)
        dbgc("a0rstd", r0[:], "a0_rstd")
        dbgc("a0nmr", n0[:], "a0_nmr")
        dbgc("a0mu", tiny["a0_mu"][:], "a0_mu")
        dbgc("a0sumsq", tiny["a0_sumsq"][:], "a0_sumsq")
        ln_apply(p, T["ln0"][:], "ln0", xti[:], xkey, r0, n0, h0)
        dbg("ln0", T["ln0"][:], "ln0")
        r1, n1, h1 = ln_stats(p, T["ln0"][:], "ln0", T["scrt"][:], "scrt", tiny, "a1", D)
        ln_apply(p, T["hgt"][:], "hgt", T["ln0"][:], "ln0", r1, n1, h1)
        p.op("ve", lambda e: e.tensor_mul(T["hg"][:], T["hgt"][:], bc["g0"][:]),
             reads=["hgt", "bc_g0"], writes=["hg"])
        dbg("hgt", T["hgt"][:], "hgt")
        transpose8(T["hg"], "hg", T["hgT"], "hgT")
        mm_chain("ps_q", ps_q, T["hgT"], "hgT", "Wq", "q")
        p.op("ve", lambda e: e.tensor_copy(T["qkf"][:], ps_q[:, :]), reads=["ps_q"], writes=["qkf"])
        dbg("qf", T["qkf"][:], "qkf")
        mm_chain("ps_s", ps_s, T["hgT"], "hgT", "Wqa2", "sq")
        softmax("ps_s", ps_s, T["qwkw"], "qwkw", "nmx", "nmxs", "esum", "rs")
        dbg("qwe", T["qwkw"][:], "qwkw")
        p.op("ve", lambda e: e.tensor_scalar(T["qwkw"][:], T["qwkw"][:], sm["rs"][:], None, ALU.mult),
             reads=["qwkw"], hard=["sm_rs"], writes=["qwkw"])
        p.op("ve", lambda e: e.tensor_mul(T["ppre"][:], T["qwkw"][:], T["qkf"][:]),
             reads=["qwkw", "qkf"], writes=["ppre"])
        dbg("ppre", T["ppre"][:], "ppre")
        if j == 0:
            p.op("gp", lambda e: e.memset(carry_q[:], 0.0), writes=["carry_q"])
            p.op("gp", lambda e: e.memset(carry_k[:], 0.0), writes=["carry_k"])
        for c in range(2):
            p.op("pe", lambda e, c=c: e.matmul(ps_c[:, c * 512:(c + 1) * 512], ones_row_b[0:1, :],
                                               carry_q[0:1, c * 512:(c + 1) * 512],
                                               start=True, stop=False),
                 reads=["carry_q", "ones_row_b"], writes=["ps_c"])
            p.op("pe", lambda e, c=c: e.matmul(ps_c[:, c * 512:(c + 1) * 512], tri_b[:],
                                               T["ppre"][:, c * 512:(c + 1) * 512],
                                               start=False, stop=True),
                 reads=["tri_b", "ppre"], writes=["ps_c"])
        p.op("ve", lambda e: e.tensor_copy(T["cqck"][:], ps_c[:, :]), reads=["ps_c"], writes=["cqck"])
        p.dma("sp", carry_f[:], T["cqck"][127:128, :], reads=["cqck"], writes=["carry_f"])
        p.op("act", lambda e: e.activation(carry_q[:], carry_f[:], AF.Copy),
             reads=["carry_f"], writes=["carry_q"])
        p.op("ve", lambda e, j=j: e.tensor_scalar(T["pqpk"][:], T["cqck"][:], invt[:, j:j + 1],
                                                  None, ALU.mult),
             reads=["cqck"], hard=["invt"], writes=["pqpk"])
        dbg("pq", T["pqpk"][:], "pqpk")
        mm_chain("ps_q", ps_q, T["hgT"], "hgT", "Wk", "k")
        p.op("ve", lambda e: e.tensor_copy(T["qkf"][:], ps_q[:, :]), reads=["ps_q"], writes=["qkf"])
        mm_chain("ps_s", ps_s, T["hgT"], "hgT", "Wv", "v")
        p.op("ve", lambda e: e.tensor_copy(T["vf"][:], ps_s[:, :]), reads=["ps_s"], writes=["vf"])
        p.op("ve", lambda e: e.tensor_mul(T["mixedf"][:], T["pqpk"][:], T["qkf"][:]),
             reads=["pqpk", "qkf"], writes=["mixedf"])
        dbg("mixed", T["mixedf"][:], "mixedf")
        transpose8(T["mixedf"], "mixedf", T["mixT"], "mixT", ident=ident_f, ident_key="ident_f",
                   ps=ps_c, ps_key="ps_c")
        mm_chain("ps_q", ps_q, T["mixT"], "mixT", "Wka", "ka")
        softmax("ps_q", ps_q, T["qwkw"], "qwkw", "nkx", "nkxs", "ksum", "rk")
        p.op("ve", lambda e: e.tensor_scalar(T["qwkw"][:], T["qwkw"][:], sm["rk"][:], None, ALU.mult),
             reads=["qwkw"], hard=["sm_rk"], writes=["qwkw"])
        p.op("ve", lambda e: e.tensor_mul(T["kpre"][:], T["qwkw"][:], T["mixedf"][:]),
             reads=["qwkw", "mixedf"], writes=["kpre"])
        for c in range(2):
            p.op("pe", lambda e, c=c: e.matmul(ps_c[:, c * 512:(c + 1) * 512], ones_row_b[0:1, :],
                                               carry_k[0:1, c * 512:(c + 1) * 512],
                                               start=True, stop=False),
                 reads=["carry_k", "ones_row_b"], writes=["ps_c"])
            p.op("pe", lambda e, c=c: e.matmul(ps_c[:, c * 512:(c + 1) * 512], tri_b[:],
                                               T["kpre"][:, c * 512:(c + 1) * 512],
                                               start=False, stop=True),
                 reads=["tri_b", "kpre"], writes=["ps_c"])
        p.op("ve", lambda e: e.tensor_copy(T["cqck"][:], ps_c[:, :]), reads=["ps_c"], writes=["cqck"])
        p.dma("sp", carry_f[:], T["cqck"][127:128, :], reads=["cqck"], writes=["carry_f"])
        p.op("act", lambda e: e.activation(carry_k[:], carry_f[:], AF.Copy),
             reads=["carry_f"], writes=["carry_k"])
        p.op("ve", lambda e, j=j: e.tensor_scalar(T["pqpk"][:], T["cqck"][:], invt[:, j:j + 1],
                                                  None, ALU.mult),
             reads=["cqck"], hard=["invt"], writes=["pqpk"])
        p.op("ve", lambda e: e.tensor_mul(T["qwkw"][:], T["pqpk"][:], T["vf"][:]),
             reads=["pqpk", "vf"], writes=["qwkw"])
        dbg("attn", T["qwkw"][:], "qwkw")
        r2, n2, h2 = ln_stats(p, T["qwkw"][:], "qwkw", T["scrt"][:], "scrt", tiny, "a2", D)
        ln_apply(p, T["mixedf"][:], "mixedf", T["qwkw"][:], "qwkw", r2, n2, h2)
        p.op("ve", lambda e, xti=xti: e.tensor_add(T["vf"][:], xti[:], T["mixedf"][:]),
             reads=[xkey, "mixedf"], writes=["vf"])
        p.dma("sp", x1_h[i * 128:(i + 1) * 128, :], T["vf"][:], reads=["vf"], writes=[("x1_h", i)])
        r3, n3, h3 = ln_stats(p, T["vf"][:], "vf", T["scrt"][:], "scrt", tiny, "a3", D)
        ln_apply(p, T["ln0"][:], "ln0", T["vf"][:], "vf", r3, n3, h3)
        r4, n4, h4 = ln_stats(p, T["ln0"][:], "ln0", T["scrt"][:], "scrt", tiny, "a4", D)
        ln_apply(p, T["hgt"][:], "hgt", T["ln0"][:], "ln0", r4, n4, h4)
        p.op("ve", lambda e: e.tensor_mul(T["hgt"][:], T["hgt"][:], bc["gff"][:]),
             reads=["hgt", "bc_gff"], writes=["hgt"])
        p.op("ve", lambda e: e.tensor_add(T["hgt"][:], T["hgt"][:], bc["bff"][:]),
             reads=["hgt", "bc_bff"], writes=["hgt"])
        p.op("ve", lambda e: e.reduce_max(sm["am"][:], T["hgt"][:], axis=AX,
                                          apply_absolute_value=True),
             reads=["hgt"], writes=["sm_am"])
        p.op("ve", lambda e: e.tensor_max(absrun[:], absrun[:], sm["am"][:]),
             reads=["absrun", "sm_am"], writes=["absrun"])
        p.dma("sp", xn1_h[i * 128:(i + 1) * 128, :], T["hgt"][:], reads=["hgt"], writes=[("xn1_h", i)])


def build_phase_b(nc, p, stB, wffT, xn1_h, x1_h, x2_h, xn_h, xnT_h, bc,
                  ident_b, gmax_bc, gb_last, qscale, alpha_s_bc, dbg_h):
    wt_b = stB.enter_context(nc.sbuf_tensor("wt_b", [128, DCH * D], BF16))
    wldB = [stB.enter_context(nc.sbuf_tensor(f"wldB{i}", [128, D], F32)) for i in range(2)]
    wabs = stB.enter_context(nc.sbuf_tensor("wabs", [128, D], F32))
    wsgn = stB.enter_context(nc.sbuf_tensor("wsgn", [128, D], F32))
    wmask = stB.enter_context(nc.sbuf_tensor("wmask", [128, D], F32))
    wsum = stB.enter_context(nc.sbuf_tensor("wsum", [128, 1], F32))
    wacc = stB.enter_context(nc.sbuf_tensor("wacc", [128, 4], F32))
    wrow = stB.enter_context(nc.sbuf_tensor("wrow", [1, 128], F32))
    wsc = stB.enter_context(nc.sbuf_tensor("wsc", [1, 4], F32))
    delta_bc = stB.enter_context(nc.sbuf_tensor("delta_bc", [128, 1], F32))
    ps_f = stB.enter_context(nc.psum_tensor("ps_f", [128, 1024], F32))
    ps_tB = stB.enter_context(nc.psum_tensor("ps_tB", [128, 1024], BF16))
    tinyB = alloc_tiny(nc, stB, ["b0", "b1", "b2"], p)

    p.op("gp", lambda e: e.memset(wacc[:], 0.0), writes=["wacc"])
    for kc in range(DCH):
        sl = kc % 2
        p.dma("sp", wldB[sl][:], wffT[kc * 128:(kc + 1) * 128, :], writes=[("wldB", sl)])
        p.op("act", lambda e, sl=sl: e.activation(wabs[:], wldB[sl][:], AF.Abs),
             reads=[("wldB", sl)], writes=["wabs"])
        p.op("ve", lambda e: e.reduce_sum(wsum[:], wabs[:], axis=AX), reads=["wabs"], writes=["wsum"])
        p.op("ve", lambda e: e.tensor_add(wacc[:, 0:1], wacc[:, 0:1], wsum[:]),
             reads=["wacc", "wsum"], writes=["wacc"])
    p.dma("sp", wrow[0:1, :], wacc[:, 0:1],
          reads=["wacc"], writes=["wrow"], slow=True)
    p.op("ve", lambda e: e.reduce_sum(wsc[0:1, 0:1], wrow[:], axis=AX), reads=["wrow"], writes=["wsc0"])
    p.op("ve", lambda e: e.tensor_scalar(wsc[0:1, 1:2], wsc[0:1, 0:1], 0.7 / (D * D), None, ALU.mult),
         reads=["wsc0"], writes=["wsc_d"])
    db_last = dma_bcast_scalar(p, delta_bc, "db", wsc[0:1, 1:2], "wsc_d")
    p.op("gp", lambda e: e.memset(wacc[:], 0.0), reads=["wacc"], writes=["wacc"])
    for kc in range(DCH):
        sl = kc % 2
        p.dma("sp", wldB[sl][:], wffT[kc * 128:(kc + 1) * 128, :], writes=[("wldB", sl)])
        p.op("act", lambda e, sl=sl: e.activation(wabs[:], wldB[sl][:], AF.Abs),
             reads=[("wldB", sl)], writes=["wabs"])
        p.op("act", lambda e, sl=sl: e.activation(wsgn[:], wldB[sl][:], AF.Sign),
             reads=[("wldB", sl)], writes=["wsgn"])
        p.op("ve", lambda e: e.tensor_scalar(wmask[:], wabs[:], delta_bc[:], None, ALU.is_gt),
             reads=["wabs"], hard=[db_last], writes=["wmask"])
        p.op("ve", lambda e, kc=kc: e.tensor_mul(wt_b[:, kc * D:(kc + 1) * D], wsgn[:], wmask[:]),
             reads=["wsgn", "wmask"], writes=[("wt", kc)])
        p.op("ve", lambda e: e.tensor_mul(wabs[:], wabs[:], wmask[:]),
             reads=["wabs", "wmask"], writes=["wabs"])
        p.op("ve", lambda e: e.reduce_sum(wsum[:], wabs[:], axis=AX), reads=["wabs"], writes=["wsum"])
        p.op("ve", lambda e: e.tensor_add(wacc[:, 1:2], wacc[:, 1:2], wsum[:]),
             reads=["wacc", "wsum"], writes=["wacc"])
        p.op("ve", lambda e: e.reduce_sum(wsum[:], wmask[:], axis=AX), reads=["wmask"], writes=["wsum"])
        p.op("ve", lambda e: e.tensor_add(wacc[:, 2:3], wacc[:, 2:3], wsum[:]),
             reads=["wacc", "wsum"], writes=["wacc"])
    p.dma("sp", wrow[0:1, :], wacc[:, 1:2],
          reads=["wacc"], writes=["wrow"], slow=True)
    p.op("ve", lambda e: e.reduce_sum(wsc[0:1, 2:3], wrow[:], axis=AX), reads=["wrow"], writes=["wsc_ma"])
    p.dma("sp", wrow[0:1, :], wacc[:, 2:3],
          reads=["wacc", "wsc_ma"], writes=["wrow"], slow=True)
    p.op("ve", lambda e: e.reduce_sum(wsc[0:1, 3:4], wrow[:], axis=AX), reads=["wrow"], writes=["wsc_m"])
    p.op("ve", lambda e: e.tensor_scalar(wsc[0:1, 3:4], wsc[0:1, 3:4], 1.0, None, ALU.max),
         reads=["wsc_m"], writes=["wsc_m"])
    p.op("ve", lambda e: e.reciprocal(wsc[0:1, 0:1], wsc[0:1, 3:4]), reads=["wsc_m"], writes=["wsc_r"])
    p.op("ve", lambda e: e.tensor_mul(wsc[0:1, 1:2], wsc[0:1, 2:3], wsc[0:1, 0:1]),
         reads=["wsc_ma", "wsc_r"], writes=["wsc_alpha"])
    p.op("ve", lambda e: e.tensor_mul(wsc[0:1, 1:2], wsc[0:1, 1:2], qscale[0:1, 0:1]),
         reads=["wsc_alpha", "qscale_s"], writes=["wsc_as"])
    ab_last = dma_bcast_scalar(p, alpha_s_bc, "ab", wsc[0:1, 1:2], "wsc_as")

    TB = {}
    for nm in ("scrB", "ffo", "lnf", "x1r", "x2t", "gt", "xnm"):
        TB[nm] = stB.enter_context(nc.sbuf_tensor(nm, [128, D], F32))
    xq32 = stB.enter_context(nc.sbuf_tensor("xq32", [128, D], I32))
    xqb = stB.enter_context(nc.sbuf_tensor("xqb", [128, D], BF16))
    xqT = stB.enter_context(nc.sbuf_tensor("xqT", [128, D], BF16))
    xnt = [stB.enter_context(nc.sbuf_tensor(f"xnt{i}", [128, D], F32)) for i in range(2)]

    for i in range(NT):
        sl = i % 2

        def dbg(nm, ap, key, i=i, slow=False):
            if _DBG == nm:
                p.dma("sp", dbg_h[i * 128:(i + 1) * 128, 0:1] if slow else dbg_h[i * 128:(i + 1) * 128, :],
                      ap, reads=[key], writes=[("dbg_h", i)], slow=slow)

        p.dma("sp", xnt[sl][:], xn1_h[i * 128:(i + 1) * 128, :],
              reads=[("xn1_h", i)], writes=[("xnt", sl)])
        p.op("ve", lambda e, sl=sl: e.tensor_scalar(TB["scrB"][:], xnt[sl][:], gmax_bc[:],
                                                    None, ALU.mult),
             reads=[("xnt", sl)], hard=[gb_last], writes=["scrB"])
        p.op("ve", lambda e: e.tensor_scalar(TB["scrB"][:], TB["scrB"][:], 127.0, -127.0,
                                             ALU.min, ALU.max),
             reads=["scrB"], writes=["scrB"])
        p.op("ve", lambda e: e.tensor_copy(xq32[:], TB["scrB"][:]), reads=["scrB"], writes=["xq32"])
        p.op("ve", lambda e: e.tensor_copy(xqb[:], xq32[:]), reads=["xq32"], writes=["xqb"])
        if _DBG == "xq":
            p.op("ve", lambda e: e.tensor_copy(TB["scrB"][:], xq32[:]), reads=["xq32"], writes=["scrB"])
            dbg("xq", TB["scrB"][:], "scrB")
        for jj in range(DCH):
            p.op("pe", lambda e, jj=jj: e.transpose(ps_tB[:, jj * 128:(jj + 1) * 128],
                                                    xqb[:, jj * 128:(jj + 1) * 128], ident_b[:]),
                 reads=["xqb", "ident_b"], writes=["ps_tB"])
        p.op("act", lambda e: e.activation(xqT[:], ps_tB[:, :], AF.Copy), reads=["ps_tB"], writes=["xqT"])
        for c in range(2):
            for kc in range(DCH):
                p.op("pe", lambda e, c=c, kc=kc: e.matmul(
                    ps_f[:, c * 512:(c + 1) * 512], xqT[:, kc * 128:(kc + 1) * 128],
                    wt_b[:, kc * D + c * 512:kc * D + (c + 1) * 512],
                    start=(kc == 0), stop=(kc == DCH - 1)),
                    reads=["xqT", ("wt", kc)], writes=["ps_f"])
        p.op("act", lambda e: e.activation(TB["ffo"][:], ps_f[:, :], AF.Identity, scale=alpha_s_bc[:]),
             reads=["ps_f"], hard=[ab_last], writes=["ffo"])
        dbg("ffo", TB["ffo"][:], "ffo")
        dbg("qsc", gmax_bc[:], "qscale_i", slow=True)
        dbg("asc", alpha_s_bc[:], "wsc_as", slow=True)
        r0, n0, h0 = ln_stats(p, TB["ffo"][:], "ffo", TB["scrB"][:], "scrB", tinyB, "b0", D)
        ln_apply(p, TB["lnf"][:], "lnf", TB["ffo"][:], "ffo", r0, n0, h0)
        p.dma("sp", TB["x1r"][:], x1_h[i * 128:(i + 1) * 128, :], reads=[("x1_h", i)], writes=["x1r"])
        p.op("ve", lambda e: e.tensor_add(TB["x2t"][:], TB["x1r"][:], TB["lnf"][:]),
             reads=["x1r", "lnf"], writes=["x2t"])
        p.dma("sp", x2_h[i * 128:(i + 1) * 128, :], TB["x2t"][:], reads=["x2t"], writes=[("x2_h", i)])
        r1, n1, h1 = ln_stats(p, TB["x2t"][:], "x2t", TB["scrB"][:], "scrB", tinyB, "b1", D)
        ln_apply(p, TB["gt"][:], "gt", TB["x2t"][:], "x2t", r1, n1, h1)
        r2, n2, h2 = ln_stats(p, TB["gt"][:], "gt", TB["scrB"][:], "scrB", tinyB, "b2", D)
        ln_apply(p, TB["xnm"][:], "xnm", TB["gt"][:], "gt", r2, n2, h2)
        p.op("ve", lambda e: e.tensor_mul(TB["xnm"][:], TB["xnm"][:], bc["ln1w"][:]),
             reads=["xnm", "bc_ln1w"], writes=["xnm"])
        p.op("ve", lambda e: e.tensor_add(TB["xnm"][:], TB["xnm"][:], bc["ln1b"][:]),
             reads=["xnm", "bc_ln1b"], writes=["xnm"])
        p.dma("sp", xn_h[i * 128:(i + 1) * 128, :], TB["xnm"][:], reads=["xnm"], writes=[("xn_h", i)])
        p.op("ve", lambda e: e.tensor_copy(xqb[:], TB["xnm"][:]), reads=["xnm"], writes=["xqb"])
        for jj in range(DCH):
            p.op("pe", lambda e, jj=jj: e.transpose(ps_tB[:, jj * 128:(jj + 1) * 128],
                                                    xqb[:, jj * 128:(jj + 1) * 128], ident_b[:]),
                 reads=["xqb", "ident_b"], writes=["ps_tB"])
        p.op("act", lambda e: e.activation(xqT[:], ps_tB[:, :], AF.Copy), reads=["ps_tB"], writes=["xqT"])
        bb = i // NTB
        tt = (i % NTB) * 128
        for jj in range(DCH):
            p.dma("sp", xnT_h[bb, jj * 128:(jj + 1) * 128, tt:tt + 128],
                  xqT[:, jj * 128:(jj + 1) * 128],
                  reads=["xqT"], writes=[("xnT_h", i, jj)])


def build_phase_c(nc, p, stack, mW, A_sb, ident_b, ident_f, tri_f, bc,
                  xnT_h, xn_h, x2_h, xc_h, z_h, ybf_h, xcb_h, xm_h, x3_h, xn3T_h):
    stC = ExitStack()
    sel_b = stC.enter_context(nc.sbuf_tensor("sel_b", [128, NST * 128], BF16))
    stSel = ExitStack()
    onesel = stSel.enter_context(nc.sbuf_tensor("onesel", [128, NST * 128], BF16))
    p.op("gp", lambda e: e.memset(onesel[:], 1.0), writes=["onesel"])
    p.op("gp", lambda e: e.affine_select(
        sel_b[:].rearrange("p (n j) -> p n j", n=NST), onesel[:].rearrange("p (n j) -> p n j", n=NST),
        [[-1, NST], [0, 128]], ALU.is_equal, 0.0, base=-64, channel_multiplier=1),
        reads=["onesel"], writes=["sel_b"])
    p.barrier()
    stSel.close()
    ps_w = stC.enter_context(nc.psum_tensor("ps_cw", [128, 1024], F32))
    ps_b2 = stC.enter_context(nc.psum_tensor("ps_cb", [128, 1024], F32))
    ps_y = stC.enter_context(nc.psum_tensor("ps_cy", [128, 2048], F32))
    xnT_sb = stC.enter_context(nc.sbuf_tensor("c_xnT", [128, DCH * L], BF16))
    global _DLB, _CC, _WXP
    _DLB = stC.enter_context(nc.sbuf_tensor("c_dlBg", [128, L], BF16))
    _CC = stC.enter_context(nc.sbuf_tensor("c_Cg", [128, L], BF16))
    _WXP = stC.enter_context(nc.sbuf_tensor("c_wxpg", [128, ECH * 192], BF16))
    p.op("gp", lambda e: e.memset(_CC[0:64, :], 0.0), writes=["c_C"])

    for b in range(BL):
        for kc in range(DCH):
            p.dma("sp", xnT_sb[:, kc * L:(kc + 1) * L], xnT_h[b, kc * 128:(kc + 1) * 128, :],
                  writes=[("cxnT", kc)])
        for dr in range(2):
            _c_stage1(nc, p, mW, dr, (dr == 0), xnT_sb, ps_w, xc_h, z_h, xcb_h)
            _c_stage2(nc, p, mW, dr, (dr == 0), A_sb, sel_b, ident_b,
                      ps_w, ps_b2, ps_y, xc_h, z_h, ybf_h)
            _c_stage3(nc, p, mW, dr, ps_w, ps_b2, ps_y, ybf_h, xm_h)
        _c_stage4(nc, p, b, bc, ident_f, ps_y, xm_h, xn_h, x2_h, x3_h, xn3T_h)
    stC.close()


def _c_stage1(nc, p, mW, dr, fwd, xnT_sb, ps_w, xc_h, z_h, xcb_h):
    st1 = ExitStack()
    xcb1 = _sbu(nc, st1, "c_xcb1", [128, L], BF16)
    xcs = [_sbu(nc, st1, f"c_xcs{i}", [128, 512], BF16) for i in range(2)]
    wch_f = _sbu(nc, st1, "c_wchf", [128, 1024], F32)
    wch_b = _sbu(nc, st1, "c_wchb", [128, 1024], BF16)
    xp_t = _sbu(nc, st1, "c_xp", [128, L + 3], F32)
    cacc = [_sbu(nc, st1, f"c_cacc{i}", [128, L], F32) for i in range(2)]
    convw_t = _sbu(nc, st1, "c_convw", [128, 4], F32)
    convb_t = _sbu(nc, st1, "c_convb", [128, 1], F32)
    xcf1 = _sbu(nc, st1, "c_xcf1", [128, L], F32)
    zf1 = _sbu(nc, st1, "c_zf1", [128, L], F32)
    pad_off = 3 if fwd else 0
    if fwd:
        p.op("gp", lambda e: e.memset(xp_t[:, 0:3], 0.0), writes=["xp_pad"])
    else:
        p.op("gp", lambda e: e.memset(xp_t[:, L:L + 3], 0.0), writes=["xp_pad"])
    for ec in range(ECH):
        p.dma("sp", convw_t[:], mW["convw"][dr][ec * 128:(ec + 1) * 128, :], writes=["convw"])
        p.dma("sp", convb_t[:], mW["convb"][dr][ec * 128:(ec + 1) * 128, :], writes=["convb"])
        for half in range(2):
            jbase = half * E + ec * 128
            p.dma("gp", wch_f[:].rearrange("p (kc j) -> p kc j", kc=DCH),
                  mW["WinT"][dr][0:D, jbase:jbase + 128].rearrange("(kc p) j -> p kc j", p=128),
                  writes=["cwf"])
            p.op("act", lambda e: e.activation(wch_b[:], wch_f[:], AF.Copy),
                 reads=["cwf"], writes=["cwb"])
            for tc in range(4):
                def mmfn(e, kc=None, tc=tc):
                    pass
                for kc in range(DCH):
                    p.op("pe", lambda e, kc=kc, tc=tc: e.matmul(
                        ps_w[:, (tc % 2) * 512:((tc % 2) + 1) * 512],
                        wch_b[:, kc * 128:(kc + 1) * 128],
                        xnT_sb[:, kc * L + tc * 512:kc * L + (tc + 1) * 512],
                        start=(kc == 0), stop=(kc == DCH - 1)),
                        reads=["cwb", ("cxnT", kc)], writes=["ps_cwALL"])
                if half == 0:
                    p.op("act", lambda e, tc=tc: e.activation(
                        xp_t[:, pad_off + tc * 512: pad_off + (tc + 1) * 512],
                        ps_w[:, (tc % 2) * 512:((tc % 2) + 1) * 512], AF.Copy),
                        reads=["ps_cwALL"], writes=["c_xp"])
                else:
                    p.op("act", lambda e, tc=tc: e.activation(
                        zf1[:, tc * 512:(tc + 1) * 512],
                        ps_w[:, (tc % 2) * 512:((tc % 2) + 1) * 512], AF.Copy),
                        reads=["ps_cwALL"], writes=["c_zf1"])
            if half == 1:
                p.dma("sp", z_h[ec * 128:(ec + 1) * 128, :], zf1[:],
                      reads=["c_zf1"], writes=[("z_h", ec)])
        offs = [k if fwd else 3 - k for k in range(4)]
        p.op("ve", lambda e, o=offs[0]: e.tensor_scalar(
            cacc[0][:], xp_t[:, o:o + L], convw_t[:, 0:1], None, ALU.mult),
            reads=["c_xp", "xp_pad"], hard=["convw"], writes=["cacc0"])
        for k in (1, 2, 3):
            src, dst = ("cacc0", "cacc1") if k % 2 == 1 else ("cacc1", "cacc0")
            ss, dd = (cacc[0], cacc[1]) if k % 2 == 1 else (cacc[1], cacc[0])
            p.op("ve", lambda e, o=offs[k], k=k, ss=ss, dd=dd: e.scalar_tensor_tensor(
                dd[:], xp_t[:, o:o + L], convw_t[:, k:k + 1], ss[:], ALU.mult, ALU.add),
                reads=["c_xp", src], hard=["convw"], writes=[dst])
        p.op("act", lambda e: e.activation(xcf1[:], cacc[1][:], AF.Silu, bias=convb_t[:]),
             reads=["cacc1"], hard=["convb"], writes=["c_xcf1"])
        p.dma("sp", xc_h[ec * 128:(ec + 1) * 128, :], xcf1[:],
              reads=["c_xcf1"], writes=[("xc_h", ec)])
        p.op("ve", lambda e: e.tensor_copy(xcb1[:], xcf1[:]),
             reads=["c_xcf1"], writes=["c_xcb1"])
        p.dma("sp", xcb_h[ec * 128:(ec + 1) * 128, :], xcb1[:],
              reads=["c_xcb1"], writes=[("xcb_h", ec)])
    global _DLB, _CC
    for ec in range(ECH):
        p.dma("sp", wch_f[:, 0:192], mW["WxpT"][dr][ec * 128:(ec + 1) * 128, :], writes=["cwf"])
        p.op("act", lambda e, ec=ec: e.activation(_WXP[:, ec * 192:(ec + 1) * 192],
                                                  wch_f[:, 0:192], AF.Copy),
             reads=["cwf"], writes=[("wxp", ec)])
    for tc in range(4):
        for ec in range(ECH):
            sl = ec % 2
            p.dma("sp", xcs[sl][:], xcb_h[ec * 128:(ec + 1) * 128, tc * 512:(tc + 1) * 512],
                  reads=[("xcb_h", ec)], writes=[("xcs", sl)])
            p.op("pe", lambda e, tc=tc, ec=ec, sl=sl: e.matmul(
                ps_w[:, 0:512], _WXP[:, ec * 192:ec * 192 + 128],
                xcs[sl][:], start=(ec == 0), stop=(ec == ECH - 1), skip_group_check=True),
                reads=[("wxp", ec), ("xcs", sl)], writes=["ps_cwALL"])
            p.op("pe", lambda e, tc=tc, ec=ec, sl=sl: e.matmul(
                ps_w[0:64, 512:1024], _WXP[:, ec * 192 + 128:ec * 192 + 192],
                xcs[sl][:], start=(ec == 0), stop=(ec == ECH - 1), skip_group_check=True),
                reads=[("wxp", ec), ("xcs", sl)], writes=["ps_cwALL2"])
        p.op("act", lambda e, tc=tc: e.activation(_DLB[:, tc * 512:(tc + 1) * 512],
                                                  ps_w[:, 0:512], AF.Copy),
             reads=["ps_cwALL"], writes=["c_dlB"])
        p.op("act", lambda e, tc=tc: e.activation(_CC[64:128, tc * 512:(tc + 1) * 512],
                                                  ps_w[0:64, 512:1024], AF.Copy),
             reads=["ps_cwALL2"], writes=["c_C"])
    p.barrier()
    st1.close()


def _c_stage2(nc, p, mW, dr, fwd, A_sb, sel_b, ident_b, ps_w, ps_b2, ps_y,
              xc_h, z_h, ybf_h):
    st2 = ExitStack()
    wdt_f = _sbu(nc, st2, "c_wdtf", [64, E], F32)
    wdt_b = _sbu(nc, st2, "c_wdtb", [64, E], BF16)
    dtb_t = _sbu(nc, st2, "c_dtb", [128, 1], F32)
    dp_t = _sbu(nc, st2, "c_dp", [128, 1], F32)
    dt_t = _sbu(nc, st2, "c_dt", [128, L], F32)
    u_t = _sbu(nc, st2, "c_u", [128, L], F32)
    xcf = _sbu(nc, st2, "c_xcf", [128, L], F32)
    zf = _sbu(nc, st2, "c_zf", [128, L], F32)
    y2 = _sbu(nc, st2, "c_y2", [128, L], F32)
    yf = _sbu(nc, st2, "c_yf", [128, L], F32)
    ybt = _sbu(nc, st2, "c_ybt", [128, L], BF16)
    dA = [_sbu(nc, st2, f"c_dA{i}", [128, L], F32) for i in range(2)]
    add_t = _sbu(nc, st2, "c_add", [128, L], F32)
    h_t = _sbu(nc, st2, "c_h", [128, L], F32)
    tmp_t = [_sbu(nc, st2, f"c_tmp{i}", [128, L], BF16) for i in range(2)]
    spz = _sbu(nc, st2, "c_spz", [128, 512], F32)
    spt = _sbu(nc, st2, "c_spt", [128, 512], F32)
    spr = _sbu(nc, st2, "c_spr", [128, 512], F32)
    p.dma("sp", wdt_f[:], mW["WdtT"][dr], writes=["wdtf"])
    p.op("act", lambda e: e.activation(wdt_b[:], wdt_f[:], AF.Copy), reads=["wdtf"], writes=["wdtb"])
    for ec in range(ECH):
        _c_scan_ec(nc, p, mW, dr, fwd, ec, A_sb, sel_b, ident_b, ps_w, ps_b2, ps_y,
                   xc_h, z_h, ybf_h, wdt_b, dtb_t, dp_t, dt_t, u_t, xcf, zf, y2, yf,
                   ybt, dA, add_t, h_t, tmp_t, spz, spt, spr)
    p.barrier()
    st2.close()


def _c_scan_ec(nc, p, mW, dr, fwd, ec, A_sb, sel_b, ident_b, ps_w, ps_b2, ps_y,
               xc_h, z_h, ybf_h, wdt_b, dtb_t, dp_t, dt_t, u_t, xcf, zf, y2, yf,
               ybt, dA, add_t, h_t, tmp_t, spz, spt, spr):
    acb = (dr * ECH + ec) * NST
    p.dma("sp", dtb_t[:], mW["dtb"][dr][ec * 128:(ec + 1) * 128, :], writes=["dtb"])
    p.dma("sp", dp_t[:], mW["Dp"][dr][ec * 128:(ec + 1) * 128, :], writes=["dp"])
    p.dma("sp", xcf[:], xc_h[ec * 128:(ec + 1) * 128, :], reads=[("xc_h", ec)], writes=["c_xcf"])
    p.dma("sp", zf[:], z_h[ec * 128:(ec + 1) * 128, :], reads=[("z_h", ec)], writes=["c_zf"])
    for tc in range(4):
        p.op("pe", lambda e, tc=tc: e.matmul(
            ps_w[:, (tc % 2) * 512:((tc % 2) + 1) * 512],
            wdt_b[0:64, ec * 128:(ec + 1) * 128],
            _DLB[0:64, tc * 512:(tc + 1) * 512], start=True, stop=True,
            skip_group_check=True),
            reads=["wdtb", "c_dlB"], writes=["ps_cwALL"])
        p.op("act", lambda e, tc=tc: e.activation(
            spz[:], ps_w[:, (tc % 2) * 512:((tc % 2) + 1) * 512], AF.Identity, bias=dtb_t[:]),
            reads=["ps_cwALL"], hard=["dtb"], writes=["c_spz"])
        p.op("act", lambda e: e.activation(spt[:], spz[:], AF.Abs),
            reads=["c_spz"], writes=["c_spt"])
        p.op("act", lambda e: e.activation(spt[:], spt[:], AF.Exp, scale=-1.0),
            reads=["c_spt"], writes=["c_spt"])
        p.op("act", lambda e: e.activation(spt[:], spt[:], AF.Ln, bias=1.0),
            reads=["c_spt"], writes=["c_spt"])
        p.op("ve", lambda e: e.tensor_scalar_max(spr[:], spz[:], 0.0),
            reads=["c_spz"], writes=["c_spr"])
        p.op("ve", lambda e, tc=tc: e.tensor_add(dt_t[:, tc * 512:(tc + 1) * 512], spr[:], spt[:]),
            reads=["c_spr", "c_spt"], writes=["c_dt"])
    p.op("ve", lambda e: e.tensor_mul(u_t[:], dt_t[:], xcf[:]),
         reads=["c_dt", "c_xcf"], writes=["c_u"])
    for n in range(NST):
        sl = n % 2
        p.op("act", lambda e, sl=sl, n=n: e.activation(
            dA[sl][:], dt_t[:], AF.Exp, scale=A_sb[:, acb + n:acb + n + 1]),
            reads=["c_dt"], hard=[("A", acb)], writes=[("dA", sl)])
        for th in range(2):
            pb = ps_b2 if th == 0 else ps_w
            pbk = "ps_cb" if th == 0 else "ps_cwALL"
            for q in range(2):
                p.op("pe", lambda e, pb=pb, n=n, th=th, q=q: e.matmul(
                    pb[:, q * 512:(q + 1) * 512], sel_b[:, n * 128:(n + 1) * 128],
                    _DLB[:, th * 1024 + q * 512:th * 1024 + (q + 1) * 512],
                    start=True, stop=True, skip_group_check=True),
                    reads=["sel_b", "c_dlB"], writes=[pbk])
            p.op("ve", lambda e, pb=pb, th=th: e.tensor_mul(
                add_t[:, th * 1024:(th + 1) * 1024], u_t[:, th * 1024:(th + 1) * 1024],
                pb[:, 0:1024]),
                reads=["c_u", pbk], writes=["c_add"])
        if fwd:
            p.op("ve", lambda e, sl=sl: e.tensor_tensor_scan(
                h_t[:], dA[sl][:], add_t[:], 0.0, ALU.mult, ALU.add),
                reads=[("dA", sl), "c_add"], writes=["c_h"])
        else:
            p.op("ve", lambda e, sl=sl: e.tensor_tensor_scan(
                h_t[:, ::-1], dA[sl][:, ::-1], add_t[:, ::-1], 0.0, ALU.mult, ALU.add),
                reads=[("dA", sl), "c_add"], writes=["c_h"])
        for th in range(2):
            pb = ps_b2 if th == 0 else ps_w
            pbk = "ps_cb" if th == 0 else "ps_cwALL"
            for q in range(2):
                p.op("pe", lambda e, pb=pb, n=n, th=th, q=q: e.matmul(
                    pb[:, q * 512:(q + 1) * 512], sel_b[:, n * 128:(n + 1) * 128],
                    _CC[:, th * 1024 + q * 512:th * 1024 + (q + 1) * 512],
                    start=True, stop=True, skip_group_check=True),
                    reads=["sel_b", "c_C"], writes=[pbk])
            p.op("ve", lambda e, pb=pb, th=th, sl=sl: e.tensor_mul(
                tmp_t[sl][:, th * 1024:(th + 1) * 1024],
                h_t[:, th * 1024:(th + 1) * 1024], pb[:, 0:1024]),
                reads=["c_h", pbk], writes=[("tmp", sl, th)])
        for tc in range(4):
            p.op("pe", lambda e, tc=tc, n=n, sl=sl: e.matmul(
                ps_y[:, tc * 512:(tc + 1) * 512], ident_b[:],
                tmp_t[sl][:, tc * 512:(tc + 1) * 512],
                start=(n == 0), stop=(n == NST - 1), skip_group_check=True),
                reads=[("tmp", sl, tc // 2), "ident_b"], writes=[("ps_cy", tc)])
    p.op("act", lambda e: e.activation(yf[:], ps_y[:, :], AF.Copy),
         reads=[("ps_cy", 0), ("ps_cy", 1), ("ps_cy", 2), ("ps_cy", 3)], writes=["c_yfv"])
    p.op("ve", lambda e: e.scalar_tensor_tensor(y2[:], xcf[:], dp_t[:], yf[:], ALU.mult, ALU.add),
         reads=["c_xcf", "c_yfv"], hard=["dp"], writes=["c_y2"])
    p.op("act", lambda e: e.activation(zf[:], zf[:], AF.Silu), reads=["c_zf"], writes=["c_zf"])
    p.op("ve", lambda e: e.tensor_mul(ybt[:], y2[:], zf[:]), reads=["c_y2", "c_zf"], writes=["c_ybt"])
    p.dma("sp", ybf_h[ec * 128:(ec + 1) * 128, :], ybt[:], reads=["c_ybt"], writes=[("ybf_h", ec)])


def _c_stage3(nc, p, mW, dr, ps_w, ps_b2, ps_y, ybf_h, xm_h):
    st3 = ExitStack()
    ytc = _sbu(nc, st3, "c_ytc", [128, ECH * 512], BF16)
    wo_f = _sbu(nc, st3, "c_wof", [128, D], F32)
    wo_b = _sbu(nc, st3, "c_wob", [128, D], BF16)
    xms = _sbu(nc, st3, "c_xms", [128, 512], F32)
    xmr = _sbu(nc, st3, "c_xmr", [128, 512], F32)
    banks = [(ps_b2, 0), (ps_b2, 512), (ps_w, 0), (ps_w, 512),
             (ps_y, 0), (ps_y, 512), (ps_y, 1024), (ps_y, 1536)]
    p.barrier()
    for tc in range(4):
        for ec in range(ECH):
            p.dma("sp", ytc[:, ec * 512:(ec + 1) * 512],
                  ybf_h[ec * 128:(ec + 1) * 128, tc * 512:(tc + 1) * 512],
                  reads=[("ybf_h", ec)], writes=[("ytc", ec)])
        for ec in range(ECH):
            p.dma("gp", wo_f[:], mW["WoutT"][dr][ec * 128:(ec + 1) * 128, :], writes=["wof"])
            p.op("act", lambda e: e.activation(wo_b[:], wo_f[:], AF.Copy),
                 reads=["wof"], writes=["wob"])
            for dc in range(DCH):
                pb, off = banks[dc]
                p.op("pe", lambda e, pb=pb, off=off, dc=dc, ec=ec: e.matmul(
                    pb[:, off:off + 512], wo_b[:, dc * 128:(dc + 1) * 128],
                    ytc[:, ec * 512:(ec + 1) * 512],
                    start=(ec == 0), stop=(ec == ECH - 1), skip_group_check=True),
                    reads=["wob", ("ytc", ec)], writes=[("k7", dc)])
        for dc in range(DCH):
            pb, off = banks[dc]
            if dr == 0:
                p.op("act", lambda e, pb=pb, off=off: e.activation(
                    xms[:], pb[:, off:off + 512], AF.Copy),
                    reads=[("k7", dc)], writes=["c_xms"])
            else:
                p.dma("sp", xmr[:], xm_h[dc * 128:(dc + 1) * 128, tc * 512:(tc + 1) * 512],
                      reads=[("xm_h", dc, tc)], writes=["c_xmr"])
                p.op("ve", lambda e, pb=pb, off=off: e.tensor_add(
                    xms[:], xmr[:], pb[:, off:off + 512]),
                    reads=["c_xmr", ("k7", dc)], writes=["c_xms"])
            p.dma("sp", xm_h[dc * 128:(dc + 1) * 128, tc * 512:(tc + 1) * 512], xms[:],
                  reads=["c_xms"], writes=[("xm_h", dc, tc)])
    p.barrier()
    st3.close()


def _c_stage4(nc, p, b, bc, ident_f, ps_y, xm_h, xn_h, x2_h, x3_h, xn3T_h):
    st4 = ExitStack()
    tinyC = alloc_tiny(nc, st4, ["c0", "c1", "c2"], p)
    xmF = _sbu(nc, st4, "c8_xmF", [128, D], F32)
    xmt = _sbu(nc, st4, "c8_xmt", [128, D], F32)
    scrC = _sbu(nc, st4, "c8_scr", [128, D], F32)
    xnr = _sbu(nc, st4, "c8_xnr", [128, D], F32)
    retv = _sbu(nc, st4, "c8_ret", [128, D], F32)
    x2r = _sbu(nc, st4, "c8_x2r", [128, D], F32)
    x3t = _sbu(nc, st4, "c8_x3t", [128, D], F32)
    xn3 = _sbu(nc, st4, "c8_xn3", [128, D], F32)
    xn3F = _sbu(nc, st4, "c8_xn3F", [128, D], F32)
    for i in range(NTB):
        it = b * NTB + i
        tt = i * 128
        for dc in range(DCH):
            p.dma("sp", xmF[:, dc * 128:(dc + 1) * 128],
                  xm_h[dc * 128:(dc + 1) * 128, tt:tt + 128],
                  reads=[("xm_h", dc, tt // 512)], writes=[("xmF", dc)])
        for dc in range(DCH):
            p.op("pe", lambda e, dc=dc: e.transpose(
                ps_y[:, dc * 128:(dc + 1) * 128], xmF[:, dc * 128:(dc + 1) * 128], ident_f[:]),
                reads=[("xmF", dc), "ident_f"], writes=[("ps_cy", 0), ("ps_cy", 1)])
        p.op("act", lambda e: e.activation(xmt[:], ps_y[:, 0:1024], AF.Copy),
             reads=[("ps_cy", 0), ("ps_cy", 1)], writes=["xmt"])
        r0, n0, h0 = ln_stats(p, xmt[:], "xmt", scrC[:], "scrC", tinyC, "c0", D)
        ln_apply(p, retv[:], "retv", xmt[:], "xmt", r0, n0, h0)
        p.op("ve", lambda e: e.tensor_mul(retv[:], retv[:], bc["ln2w"][:]),
             reads=["retv", "bc_ln2w"], writes=["retv"])
        p.op("ve", lambda e: e.tensor_add(retv[:], retv[:], bc["ln2b"][:]),
             reads=["retv", "bc_ln2b"], writes=["retv"])
        p.dma("sp", xnr[:], xn_h[it * 128:(it + 1) * 128, :], reads=[("xn_h", it)], writes=["xnr"])
        p.op("ve", lambda e: e.tensor_add(retv[:], retv[:], xnr[:]),
             reads=["retv", "xnr"], writes=["retv"])
        r1, n1, h1 = ln_stats(p, retv[:], "retv", scrC[:], "scrC", tinyC, "c1", D)
        ln_apply(p, xn3[:], "lnretv", retv[:], "retv", r1, n1, h1)
        p.dma("sp", x2r[:], x2_h[it * 128:(it + 1) * 128, :], reads=[("x2_h", it)], writes=["x2r"])
        p.op("ve", lambda e: e.tensor_add(x3t[:], x2r[:], xn3[:]),
             reads=["x2r", "lnretv"], writes=["x3t"])
        p.dma("sp", x3_h[it * 128:(it + 1) * 128, :], x3t[:], reads=["x3t"], writes=[("x3_h", it)])
        r2, n2, h2 = ln_stats(p, x3t[:], "x3t", scrC[:], "scrC", tinyC, "c2", D)
        ln_apply(p, xn3[:], "xn3", x3t[:], "x3t", r2, n2, h2)
        for dc in range(DCH):
            p.op("pe", lambda e, dc=dc: e.transpose(
                ps_y[:, 1024 + dc * 128:1024 + (dc + 1) * 128],
                xn3[:, dc * 128:(dc + 1) * 128], ident_f[:]),
                reads=["xn3", "ident_f"], writes=[("ps_cy", 2), ("ps_cy", 3)])
        p.op("act", lambda e: e.activation(xn3F[:], ps_y[:, 1024:2048], AF.Copy),
             reads=[("ps_cy", 2), ("ps_cy", 3)], writes=["xn3F"])
        for dc in range(DCH):
            p.dma("sp", xn3T_h[b, dc * 128:(dc + 1) * 128, tt:tt + 128],
                  xn3F[:, dc * 128:(dc + 1) * 128],
                  reads=["xn3F"], writes=[("xn3T_h", b, dc, i)])
    p.barrier()
    st4.close()


def build_phase_d(nc, p, stack, WrT, WkkT, WvvT, murc, mukc,
                  xn3T_h, x3_h, z_h, xm_h, out_hbm, ident_f):
    stD = ExitStack()
    ps_a = stD.enter_context(nc.psum_tensor("ps_da", [128, 2048], F32))
    ps_b = stD.enter_context(nc.psum_tensor("ps_db", [128, 2048], F32))
    mur_t = _sbu(nc, stD, "d_mur", [128, DCH], F32)
    muk_t = _sbu(nc, stD, "d_muk", [128, DCH], F32)
    p.dma("sp", mur_t[:], murc.rearrange("(c p) one -> p (c one)", p=128), writes=["d_mur"], slow=True)
    p.dma("sp", muk_t[:], mukc.rearrange("(c p) one -> p (c one)", p=128), writes=["d_muk"], slow=True)
    for b in range(BL):
        _d_body(nc, p, b, WrT, WkkT, WvvT, mur_t, muk_t, xn3T_h, x3_h, z_h, xm_h,
                out_hbm, ident_f, ps_a, ps_b)
    stD.close()


def _d_body(nc, p, b, WrT, WkkT, WvvT, mur_t, muk_t, xn3T_h, x3_h, z_h, xm_h,
            out_hbm, ident_f, ps_a, ps_b):
    banksD = [(ps_a, 0), (ps_a, 512), (ps_a, 1024), (ps_a, 1536),
              (ps_b, 0), (ps_b, 512), (ps_b, 1024), (ps_b, 1536)]
    st1 = ExitStack()
    xsh = _sbu(nc, st1, "d_xsh", [128, L + 1], F32)
    dbuf = _sbu(nc, st1, "d_dbuf", [128, L], F32)
    xr_sb = _sbu(nc, st1, "d_xr", [128, DCH * L], BF16)
    xk_sb = _sbu(nc, st1, "d_xk", [128, DCH * L], BF16)
    ka_sb = _sbu(nc, st1, "d_ka", [128, DCH * L], BF16)
    wd_f = _sbu(nc, st1, "d_wf", [128, D], F32)
    wd_b = _sbu(nc, st1, "d_wb", [128, D], BF16)
    ev1 = _sbu(nc, st1, "d_ev1", [128, 512], F32)
    ev2 = _sbu(nc, st1, "d_ev2", [128, 512], F32)
    for dc in range(DCH):
        p.op("gp", lambda e: e.memset(xsh[:, 0:1], 0.0), reads=["d_xsh"], writes=["d_xsh"])
        p.dma("sp", xsh[:, 1:L + 1], xn3T_h[b, dc * 128:(dc + 1) * 128, :],
              reads=[("xn3T_h", b, dc, i2) for i2 in range(NTB)], writes=["d_xsh"])
        p.op("ve", lambda e: e.tensor_sub(dbuf[:], xsh[:, 1:L + 1], xsh[:, 0:L]),
             reads=["d_xsh"], writes=["d_dbuf"])
        p.op("ve", lambda e, dc=dc: e.scalar_tensor_tensor(
            xr_sb[:, dc * L:(dc + 1) * L], dbuf[:], mur_t[:, dc:dc + 1], xsh[:, 0:L],
            ALU.mult, ALU.add),
            reads=["d_dbuf", "d_xsh"], hard=["d_mur"], writes=[("xr", dc)])
        p.op("ve", lambda e, dc=dc: e.scalar_tensor_tensor(
            xk_sb[:, dc * L:(dc + 1) * L], dbuf[:], muk_t[:, dc:dc + 1], xsh[:, 0:L],
            ALU.mult, ALU.add),
            reads=["d_dbuf", "d_xsh"], hard=["d_muk"], writes=[("xk", dc)])
    for tc in range(4):
        for kc in range(DCH):
            p.dma("gp", wd_f[:], WkkT[kc * 128:(kc + 1) * 128, :], writes=["d_wf"])
            p.op("act", lambda e: e.activation(wd_b[:], wd_f[:], AF.Copy),
                 reads=["d_wf"], writes=["d_wb"])
            for dc in range(DCH):
                pb, off = banksD[dc]
                p.op("pe", lambda e, pb=pb, off=off, dc=dc, kc=kc, tc=tc: e.matmul(
                    pb[:, off:off + 512], wd_b[:, dc * 128:(dc + 1) * 128],
                    xk_sb[:, kc * L + tc * 512:kc * L + (tc + 1) * 512],
                    start=(kc == 0), stop=(kc == DCH - 1), skip_group_check=True),
                    reads=["d_wb", ("xk", kc)], writes=[("dbank", dc)])
        for dc in range(DCH):
            pb, off = banksD[dc]
            p.op("act", lambda e, pb=pb, off=off: e.activation(ev1[:], pb[:, off:off + 512], AF.Relu),
                 reads=[("dbank", dc)], writes=["d_ev1"])
            p.op("act", lambda e, dc=dc, tc=tc: e.activation(
                ka_sb[:, dc * L + tc * 512:dc * L + (tc + 1) * 512], ev1[:], AF.Square),
                reads=["d_ev1"], writes=[("ka", dc)])
    for tc in range(4):
        for kc in range(DCH):
            p.dma("gp", wd_f[:], WvvT[kc * 128:(kc + 1) * 128, :], writes=["d_wf"])
            p.op("act", lambda e: e.activation(wd_b[:], wd_f[:], AF.Copy),
                 reads=["d_wf"], writes=["d_wb"])
            for dc in range(DCH):
                pb, off = banksD[dc]
                p.op("pe", lambda e, pb=pb, off=off, dc=dc, kc=kc, tc=tc: e.matmul(
                    pb[:, off:off + 512], wd_b[:, dc * 128:(dc + 1) * 128],
                    ka_sb[:, kc * L + tc * 512:kc * L + (tc + 1) * 512],
                    start=(kc == 0), stop=(kc == DCH - 1), skip_group_check=True),
                    reads=["d_wb", ("ka", kc)], writes=[("dbank", dc)])
        for dc in range(DCH):
            pb, off = banksD[dc]
            p.op("act", lambda e, pb=pb, off=off: e.activation(ev1[:], pb[:, off:off + 512], AF.Copy),
                 reads=[("dbank", dc)], writes=["d_ev1"])
            p.dma("sp", z_h[dc * 128:(dc + 1) * 128, tc * 512:(tc + 1) * 512], ev1[:],
                  reads=["d_ev1"], writes=[("v2_h", dc, tc)])
    for tc in range(4):
        for kc in range(DCH):
            p.dma("gp", wd_f[:], WrT[kc * 128:(kc + 1) * 128, :], writes=["d_wf"])
            p.op("act", lambda e: e.activation(wd_b[:], wd_f[:], AF.Copy),
                 reads=["d_wf"], writes=["d_wb"])
            for dc in range(DCH):
                pb, off = banksD[dc]
                p.op("pe", lambda e, pb=pb, off=off, dc=dc, kc=kc, tc=tc: e.matmul(
                    pb[:, off:off + 512], wd_b[:, dc * 128:(dc + 1) * 128],
                    xr_sb[:, kc * L + tc * 512:kc * L + (tc + 1) * 512],
                    start=(kc == 0), stop=(kc == DCH - 1), skip_group_check=True),
                    reads=["d_wb", ("xr", kc)], writes=[("dbank", dc)])
        for dc in range(DCH):
            pb, off = banksD[dc]
            p.op("act", lambda e, pb=pb, off=off: e.activation(ev1[:], pb[:, off:off + 512], AF.Sigmoid),
                 reads=[("dbank", dc)], writes=["d_ev1"])
            p.dma("sp", ev2[:], z_h[dc * 128:(dc + 1) * 128, tc * 512:(tc + 1) * 512],
                  reads=[("v2_h", dc, tc)], writes=["d_ev2"])
            p.op("ve", lambda e: e.tensor_mul(ev2[:], ev1[:], ev2[:]),
                 reads=["d_ev1", "d_ev2"], writes=["d_ev2"])
            p.dma("sp", xm_h[dc * 128:(dc + 1) * 128, tc * 512:(tc + 1) * 512], ev2[:],
                  reads=["d_ev2"], writes=[("o_h", dc, tc)])
    p.barrier()
    st1.close()
    st2 = ExitStack()
    tinyD = alloc_tiny(nc, st2, ["d0"], p)
    oF = _sbu(nc, st2, "d_oF", [128, D], F32)
    ot = _sbu(nc, st2, "d_ot", [128, D], F32)
    scrD = _sbu(nc, st2, "d_scr", [128, D], F32)
    lnD = _sbu(nc, st2, "d_ln", [128, D], F32)
    x3r = _sbu(nc, st2, "d_x3r", [128, D], F32)
    x4 = _sbu(nc, st2, "d_x4", [128, D], F32)
    for i in range(NTB):
        it = b * NTB + i
        tt = i * 128
        for dc in range(DCH):
            p.dma("sp", oF[:, dc * 128:(dc + 1) * 128],
                  xm_h[dc * 128:(dc + 1) * 128, tt:tt + 128],
                  reads=[("o_h", dc, tt // 512)], writes=[("oF", dc)])
        for dc in range(DCH):
            p.op("pe", lambda e, dc=dc: e.transpose(
                ps_a[:, dc * 128:(dc + 1) * 128], oF[:, dc * 128:(dc + 1) * 128], ident_f[:]),
                reads=[("oF", dc), "ident_f"], writes=[("dbank", 0), ("dbank", 1)])
        p.op("act", lambda e: e.activation(ot[:], ps_a[:, 0:1024], AF.Copy),
             reads=[("dbank", 0), ("dbank", 1)], writes=["d_ot"])
        r0, n0, h0 = ln_stats(p, ot[:], "d_ot", scrD[:], "d_scr", tinyD, "d0", D)
        ln_apply(p, lnD[:], "d_ln", ot[:], "d_ot", r0, n0, h0)
        p.dma("sp", x3r[:], x3_h[it * 128:(it + 1) * 128, :], reads=[("x3_h", it)], writes=["d_x3r"])
        p.op("ve", lambda e: e.tensor_add(x4[:], x3r[:], lnD[:]),
             reads=["d_x3r", "d_ln"], writes=["d_x4"])
        p.dma("sp", out_hbm[it * 128:(it + 1) * 128, :], x4[:],
              reads=["d_x4"], writes=[("out_h", it)])
    p.barrier()
    st2.close()


def _host_inputs(inputs, core):
    ct = np.ascontiguousarray
    x = inputs["L0_attn_hidden_states"][2 * core:2 * core + 2].reshape(TOK, D)
    m = {"x_in": ct(x)}
    m["Wq"] = ct(inputs["L0_attn_Wq"]); m["WqT"] = ct(inputs["L0_attn_Wq"].T)
    m["Wk"] = ct(inputs["L0_attn_Wk"]); m["Wv"] = ct(inputs["L0_attn_Wv"])
    m["Wqa"] = ct(inputs["L0_attn_Wqa"]); m["Wka"] = ct(inputs["L0_attn_Wka"])
    for nm, key in (("bq", "L0_attn_bq"), ("bk", "L0_attn_bk"), ("bv", "L0_attn_bv"),
                    ("bqa", "L0_attn_bqa"), ("bka", "L0_attn_bka"),
                    ("g0", "L0_attn_gamma"),
                    ("gff", "L0_ffn_gamma"), ("bff", "L0_ffn_beta"),
                    ("ln1w", "L1_attn_ln1_w"), ("ln1b", "L1_attn_ln1_b"),
                    ("ln2w", "L1_attn_ln2_w"), ("ln2b", "L1_attn_ln2_b")):
        m[nm] = ct(inputs[key].reshape(1, D))
    m["be0c"] = ct(inputs["L0_attn_beta"].reshape(D, 1))
    m["wffT"] = ct(inputs["L0_ffn_weight"].T)
    for mi, pre in ((1, "L1_attn_m1_"), (2, "L1_attn_m2_")):
        m[f"m{mi}_WinT"] = ct(inputs[pre + "W_in"].T)
        m[f"m{mi}_convw"] = ct(inputs[pre + "conv_w"])
        m[f"m{mi}_convb"] = ct(inputs[pre + "conv_b"].reshape(E, 1))
        m[f"m{mi}_WxpT"] = ct(inputs[pre + "W_xp"].T)
        m[f"m{mi}_WdtT"] = ct(inputs[pre + "W_dt"].T)
        m[f"m{mi}_dtb"] = ct(inputs[pre + "dt_b"].reshape(E, 1))
        m[f"m{mi}_Alog"] = ct(inputs[pre + "A_log"])
        m[f"m{mi}_Dp"] = ct(inputs[pre + "D"].reshape(E, 1))
        m[f"m{mi}_WoutT"] = ct(inputs[pre + "W_out"].T)
    m["WrT"] = ct(inputs["L1_ffn_W_r"].T)
    m["WkkT"] = ct(inputs["L1_ffn_W_k"].T)
    m["WvvT"] = ct(inputs["L1_ffn_W_v"].T)
    m["murc"] = ct(inputs["L1_ffn_mu_r"].reshape(D, 1))
    m["mukc"] = ct(inputs["L1_ffn_mu_k"].reshape(D, 1))
    return m


def kernel(**inputs):
    key = (_PHASES, _TAP)
    if key not in _NC_CACHE:
        _NC_CACHE[key] = build()
    nc = _NC_CACHE[key]
    in_maps = [_host_inputs(inputs, c) for c in range(NCORES)]
    res = run_bass_kernel_spmd(nc, in_maps, core_ids=list(range(NCORES)))
    return np.concatenate([res.results[c]["out"].reshape(BL, L, D) for c in range(NCORES)], axis=0)


# version 02 (final)
# speedup vs baseline: 1.0859x; 1.0859x over previous
import numpy as np
import concourse.bass as bass
import concourse.mybir as mybir
from concourse.bass_utils import run_bass_kernel_spmd
from contextlib import ExitStack

F32 = mybir.dt.float32
BF16 = mybir.dt.bfloat16
I32 = mybir.dt.int32
AF = mybir.ActivationFunctionType
ALU = mybir.AluOpType
AX = mybir.AxisListType.X

NCORES = 8
BL = 2
L = 2048
D = 1024
TOK = BL * L
NT = TOK // 128
NTB = L // 128
DCH = D // 128
E = 2048
ECH = E // 128
NST = 64
EPS = 1e-5

ENG2BLOCK = {"pe": "tensor", "act": "scalar", "ve": "vector", "gp": "gpsimd", "sp": "sync"}

_UNIQ = [0]


def _sbu(nc, st, name, shape, dtype):
    _UNIQ[0] += 1
    return st.enter_context(nc.sbuf_tensor(f"{name}_u{_UNIQ[0]}", shape, dtype))

_PHASES = 4
_TAP = None
_DBG = None
_SERDMA = False


class Prog:

    def __init__(self, nc):
        self.nc = nc
        self.ops = []
        self.bufs = {}
        self.tiny = set()
        self.pending_barrier = None
        self.post_barrier_seen = set()

    def _buf(self, key):
        return self.bufs.setdefault(key, {"w": None, "r": []})

    def op(self, eng, fn, reads=(), writes=(), hard=(), deps=(), inc=1, raw_inc=False):
        o = {"i": len(self.ops), "eng": eng, "fn": fn, "deps": [], "hard": set(),
             "inc": inc, "need_inc": False, "cum": None, "raw_inc": raw_inc}
        dd = list(deps)
        hardset = set(hard) | {k for k in reads if k in self.tiny}
        for key in list(reads) + list(hard):
            st = self._buf(key)
            if st["w"] is not None:
                dd.append(st["w"])
                if key in hardset:
                    o["hard"].add(st["w"]["i"])
        for key in writes:
            st = self._buf(key)
            if st["w"] is not None:
                dd.append(st["w"])
            dd.extend(st["r"])
        if self.pending_barrier is not None and eng not in self.post_barrier_seen:
            dd.extend(self.pending_barrier)
            self.post_barrier_seen.add(eng)
        seen = {}
        for d in dd:
            if d is not o:
                seen[d["i"]] = d
        o["deps"] = list(seen.values())
        for key in list(reads) + list(hard):
            self._buf(key)["r"].append(o)
        for key in writes:
            st = self._buf(key)
            st["w"] = o
            st["r"] = []
        self.ops.append(o)
        return o

    def dma(self, eng, out_ap, in_ap, reads=(), writes=(), deps=(), slow=False):
        def fn(e, out_ap=out_ap, in_ap=in_ap, slow=slow):
            return e.dma_start(out=out_ap, in_=in_ap, allow_slow_non_contiguous=slow)
        o = self.op(eng, fn, reads=reads, writes=writes, deps=deps, inc=16)
        o["need_inc"] = True
        return o

    def barrier(self):
        last = {}
        for o in self.ops:
            last[o["eng"]] = o
        self.pending_barrier = list(last.values())
        self.post_barrier_seen = set()

    DMA_SLOTS = 24

    def emit(self, stack):
        nc = self.nc
        for o in self.ops:
            for d in o["deps"]:
                d["need_inc"] = True
                if d["i"] in o["hard"]:
                    d["xdep"] = True
        sems = {eng: stack.enter_context(nc.semaphore(f"sem_{eng}")) for eng in ENG2BLOCK}
        dsems = {}
        for eng in ("sp", "gp"):
            dsems[eng] = [stack.enter_context(nc.semaphore(f"dma_{eng}{i}"))
                          for i in range(self.DMA_SLOTS)]
        nc.all_engine_barrier()
        allsems = list(sems.values()) + [s for ss in dsems.values() for s in ss]
        nums = sorted(s.num for s in allsems)
        assert nums == list(range(nums[0], nums[0] + len(nums)))
        nc.gpsimd.dma_reset(range(nums[0], nums[-1] + 1))
        nc.gpsimd.sem_clear(range(nums[0], nums[-1] + 1))
        nc.all_engine_barrier()
        cnt = dict.fromkeys(ENG2BLOCK, 0)
        dcnt = {eng: [0] * self.DMA_SLOTS for eng in dsems}
        didx = dict.fromkeys(dsems, 0)
        for o in self.ops:
            if o["inc"] == 16:
                slot = didx[o["eng"]] % self.DMA_SLOTS
                didx[o["eng"]] += 1
                o["slot"] = slot
                o["issue_wait"] = dcnt[o["eng"]][slot] * 16
                dcnt[o["eng"]][slot] += 1
                o["cum"] = dcnt[o["eng"]][slot] * 16
                o["semobj"] = dsems[o["eng"]][slot]
                o["semkey"] = (o["eng"], slot)
            elif o["need_inc"]:
                cnt[o["eng"]] += o["inc"]
                o["cum"] = cnt[o["eng"]]
                o["semobj"] = sems[o["eng"]]
                o["semkey"] = o["eng"]
        byeng = {k: [] for k in ENG2BLOCK}
        for o in self.ops:
            byeng[o["eng"]].append(o)
        block = stack.enter_context(nc.Block())

        def make_body(eng, ops):
            def body(e):
                hwm = {}
                for o in ops:
                    if o["inc"] == 16 and o["issue_wait"] > 0:
                        k = o["semkey"]
                        if o["issue_wait"] > hwm.get(k, 0):
                            e.wait_ge(o["semobj"], o["issue_wait"])
                            hwm[k] = o["issue_wait"]
                    for d in o["deps"]:
                        if d["eng"] == eng and d["i"] not in o["hard"] and d["inc"] != 16:
                            continue
                        if d["cum"] is None:
                            continue
                        k = d["semkey"]
                        if d["cum"] > hwm.get(k, 0):
                            e.wait_ge(d["semobj"], d["cum"])
                            hwm[k] = d["cum"]
                    inst = o["fn"](e)
                    if o["inc"] == 16:
                        inst.then_inc(o["semobj"], 16)
                    elif o["need_inc"]:
                        if o.get("xdep") and not o["raw_inc"] and eng in ("ve", "act", "gp"):
                            e.drain().then_inc(o["semobj"], o["inc"])
                        else:
                            inst.then_inc(o["semobj"], o["inc"])
            return body

        for eng, bname in ENG2BLOCK.items():
            getattr(block, bname)(make_body(eng, byeng[eng]))


def alloc_tiny(nc, st, prefixes, p=None):
    tiny = {}
    for pref in prefixes:
        for suf in ("_sums", "_sumsq", "_mu", "_msq", "_var", "_ln", "_rstd", "_nmr"):
            tiny[pref + suf] = _sbu(nc, st, pref + suf, [128, 1], F32)
            if p is not None:
                p.tiny.add(pref + suf)
    return tiny


def ln_stats(p, src_ap, src_key, scratch_ap, scratch_key, tiny, pref, n_free):
    k = pref
    sums, sumsq = tiny[k + "_sums"], tiny[k + "_sumsq"]
    t_mu, t_msq, t_var, t_ln = tiny[k + "_mu"], tiny[k + "_msq"], tiny[k + "_var"], tiny[k + "_ln"]
    rstd, nmr = tiny[k + "_rstd"], tiny[k + "_nmr"]
    p.op("ve", lambda e: e.reduce_sum(sums[:], src_ap, axis=AX),
         reads=[src_key], writes=[k + "_sums"])
    p.op("act", lambda e: e.activation(scratch_ap, src_ap, AF.Square, accum_out=sumsq[:]),
         reads=[src_key], writes=[scratch_key, k + "_sumsq"])
    p.op("ve", lambda e: e.tensor_scalar(t_mu[:], sums[:], 1.0 / n_free, None, ALU.mult),
         reads=[k + "_sums"], writes=[k + "_mu"])
    p.op("ve", lambda e: e.tensor_mul(t_msq[:], t_mu[:], t_mu[:]),
         reads=[k + "_mu"], writes=[k + "_msq"])
    p.op("ve", lambda e: e.scalar_tensor_tensor(t_var[:], sumsq[:], 1.0 / n_free, t_msq[:],
                                                ALU.mult, ALU.subtract),
         reads=[k + "_sumsq", k + "_msq"], writes=[k + "_var"])
    p.op("act", lambda e: e.activation(t_ln[:], t_var[:], AF.Ln, bias=float(EPS)),
         reads=[k + "_var"], hard=["eps_t"], writes=[k + "_ln"])
    p.op("act", lambda e: e.activation(rstd[:], t_ln[:], AF.Exp, scale=-0.5),
         reads=[k + "_ln"], writes=[k + "_rstd"])
    p.op("ve", lambda e: e.tensor_scalar(nmr[:], t_mu[:], rstd[:], -1.0, ALU.mult, ALU.mult),
         reads=[k + "_mu"], hard=[k + "_rstd"], writes=[k + "_nmr"])
    return rstd, nmr, (k + "_rstd", k + "_nmr")


def ln_apply(p, out_ap, out_key, src_ap, src_key, rstd, nmr, hard_keys):
    p.op("act", lambda e: e.activation(out_ap, src_ap, AF.Identity, scale=rstd[:], bias=nmr[:]),
         reads=[src_key], hard=list(hard_keys), writes=[out_key])


def dma_bcast_scalar(p, dst, dst_name, src_ap, src_key):
    p.dma("sp", dst[0:1, 0:1], src_ap, reads=[src_key], writes=[(dst_name, 1)])
    sz = 1
    while sz < 128:
        s2 = min(sz, 128 - sz)
        p.dma("sp", dst[sz:sz + s2, 0:1], dst[0:s2, 0:1],
              reads=[(dst_name, sz)], writes=[(dst_name, sz + s2)])
        sz += s2
    return (dst_name, 128)


_NC_CACHE = {}


def build():
    nc = bass.Bass("TRN2", target_bir_lowering=False)
    p = Prog(nc)

    def din(name, shape, dtype=F32):
        return nc.dram_tensor(name, shape, dtype, kind="ExternalInput").ap()

    x_in = din("x_in", [TOK, D])
    Wq = din("Wq", [D, D]); WqT = din("WqT", [D, D]); Wk = din("Wk", [D, D]); Wv = din("Wv", [D, D])
    Wqa = din("Wqa", [D, D]); Wka = din("Wka", [D, D])
    bq = din("bq", [1, D]); bk = din("bk", [1, D]); bv = din("bv", [1, D])
    bqa = din("bqa", [1, D]); bka = din("bka", [1, D])
    g0 = din("g0", [1, D]); be0c = din("be0c", [D, 1])
    wffT = din("wffT", [D, D]); gff = din("gff", [1, D]); bff = din("bff", [1, D])
    ln1w = din("ln1w", [1, D]); ln1b = din("ln1b", [1, D])
    ln2w = din("ln2w", [1, D]); ln2b = din("ln2b", [1, D])
    mW = {"WinT": [], "convw": [], "convb": [], "WxpT": [], "WdtT": [], "dtb": [],
          "Alog": [], "Dp": [], "WoutT": []}
    for mi in (1, 2):
        mW["WinT"].append(din(f"m{mi}_WinT", [D, 2 * E]))
        mW["convw"].append(din(f"m{mi}_convw", [E, 4]))
        mW["convb"].append(din(f"m{mi}_convb", [E, 1]))
        mW["WxpT"].append(din(f"m{mi}_WxpT", [E, 192]))
        mW["WdtT"].append(din(f"m{mi}_WdtT", [NST, E]))
        mW["dtb"].append(din(f"m{mi}_dtb", [E, 1]))
        mW["Alog"].append(din(f"m{mi}_Alog", [E, NST]))
        mW["Dp"].append(din(f"m{mi}_Dp", [E, 1]))
        mW["WoutT"].append(din(f"m{mi}_WoutT", [E, D]))
    WrT = din("WrT", [D, D]); WkkT = din("WkkT", [D, D]); WvvT = din("WvvT", [D, D])
    murc = din("murc", [D, 1]); mukc = din("mukc", [D, 1])

    out_hbm = nc.dram_tensor("out", [TOK, D], F32, kind="ExternalOutput").ap()

    def scr(name, shape, dtype=F32):
        return nc.dram_tensor(name, shape, dtype, kind="Internal").ap()

    dbg_h = scr("dbg_h", [TOK, D])
    x1_h = scr("x1_h", [TOK, D])
    xn1_h = scr("xn1_h", [TOK, D])
    x2_h = scr("x2_h", [TOK, D])
    xn_h = scr("xn_h", [TOK, D])
    xnT_h = scr("xnT_h", [BL, D, L], BF16)
    xc_h = scr("xc_h", [E, L])
    z_h = scr("z_h", [E, L])
    ybf_h = scr("ybf_h", [E, L], BF16)
    xcb_h = scr("xcb_h", [E, L], BF16)
    xm_h = scr("xm_h", [D, L])
    x3_h = scr("x3_h", [TOK, D])
    xn3T_h = scr("xn3T_h", [BL, D, L])
    cc_in_h = nc.dram_tensor("cc_in_h", [1, 1], F32).ap()
    cc_out_h = nc.dram_tensor("cc_out_h", [1, 1], F32).ap()

    scratches = {"x1_h": x1_h, "xn1_h": xn1_h, "x2_h": x2_h, "xn_h": xn_h, "x3_h": x3_h,
                 "dbg_h": dbg_h}

    with ExitStack() as stack:
        ident_f = stack.enter_context(nc.sbuf_tensor("ident_f", [128, 128], F32))
        ident_b = stack.enter_context(nc.sbuf_tensor("ident_b", [128, 128], BF16))
        tri_b = stack.enter_context(nc.sbuf_tensor("tri_b", [128, 128], BF16))
        ones_f = stack.enter_context(nc.sbuf_tensor("ones_f", [128, 128], F32))
        ones_row_b = stack.enter_context(nc.sbuf_tensor("ones_row_b", [1, 128], BF16))
        tri_f = stack.enter_context(nc.sbuf_tensor("tri_f", [128, 128], F32))
        invt = stack.enter_context(nc.sbuf_tensor("invt", [128, NTB], F32))
        A_sb = stack.enter_context(nc.sbuf_tensor("A_sb", [128, 2 * ECH * NST], F32))
        gmax_bc = stack.enter_context(nc.sbuf_tensor("gmax_bc", [128, 1], F32))
        for k in ("qscale_s", "qscale_i", "gm_sc", "gm_row", "wsum", "wacc", "wsc0",
                  "wsc_d", "wsc_ma", "wsc_m", "wsc_r", "wsc_alpha", "wsc_as",
                  "dtb", "dp", "convw", "convb", "d_mur", "d_muk"):
            p.tiny.add(k)
        qscale = stack.enter_context(nc.sbuf_tensor("qscale", [1, 2], F32))
        absrun = stack.enter_context(nc.sbuf_tensor("absrun", [128, 1], F32))
        alpha_s_bc = stack.enter_context(nc.sbuf_tensor("alpha_s_bc", [128, 1], F32))
        nopb = stack.enter_context(nc.sbuf_tensor("nopb", [1, 2], F32))
        eps_t = stack.enter_context(nc.sbuf_tensor("eps_t", [128, 1], F32))
        one_t = stack.enter_context(nc.sbuf_tensor("one_t", [128, 1], F32))
        bc = {}
        for nm in ("ln2w", "ln2b"):
            bc[nm] = stack.enter_context(nc.sbuf_tensor("bc_" + nm, [128, D], F32))
        stAB = ExitStack()
        for nm in ("g0", "gff", "bff", "ln1w", "ln1b"):
            bc[nm] = stAB.enter_context(nc.sbuf_tensor("bc_" + nm, [128, D], F32))

        nc.const_aps.register(F32, float(EPS), eps_t[:, 0:1])
        nc.const_aps.register(F32, 1.0, one_t[:, 0:1])
        p.op("gp", lambda e: e.memset(eps_t[:], float(EPS)), writes=["eps_t"])
        p.op("gp", lambda e: e.memset(one_t[:], 1.0), writes=["one_t"])
        p.op("gp", lambda e: e.memset(ones_f[:], 1.0), writes=["ones_f"])
        p.op("gp", lambda e: e.memset(ones_row_b[:], 1.0), writes=["ones_row_b"])
        p.op("gp", lambda e: e.affine_select(ident_f[:], ones_f[:], [[1, 128]], ALU.is_equal, 0.0,
                                             base=0, channel_multiplier=-1),
             reads=["ones_f"], writes=["ident_f"])
        p.op("ve", lambda e: e.tensor_copy(ident_b[:], ident_f[:]), reads=["ident_f"], writes=["ident_b"])
        p.op("gp", lambda e: e.affine_select(tri_f[:], ones_f[:], [[1, 128]], ALU.is_ge, 0.0,
                                             base=0, channel_multiplier=-1),
             reads=["ones_f"], writes=["tri_f"])
        p.op("ve", lambda e: e.tensor_copy(tri_b[:], tri_f[:]), reads=["tri_f"], writes=["tri_b"])
        p.op("gp", lambda e: e.iota(invt[:], [[128, NTB]], base=1, channel_multiplier=1,
                                    allow_small_or_imprecise_dtypes=True), writes=["invt"])
        p.op("ve", lambda e: e.reciprocal(invt[:], invt[:]), reads=["invt"], writes=["invt"])
        p.op("gp", lambda e: e.memset(absrun[:], 0.0), writes=["absrun"])

        st0 = ExitStack()
        row_f = st0.enter_context(nc.sbuf_tensor("row_f", [1, D], F32))
        ps_bc = st0.enter_context(nc.psum_tensor("ps_bc", [128, 512], F32))
        for nm, src in (("g0", g0), ("gff", gff), ("bff", bff), ("ln1w", ln1w),
                        ("ln1b", ln1b), ("ln2w", ln2w), ("ln2b", ln2b)):
            p.dma("sp", row_f[:], src, writes=["row_f"])
            for c in range(2):
                p.op("pe", lambda e, c=c: e.matmul(ps_bc[:, :], ones_f[0:1, :],
                                                   row_f[0:1, c * 512:(c + 1) * 512],
                                                   start=True, stop=True),
                     reads=["ones_f", "row_f"], writes=["ps_bc"])
                p.op("act", lambda e, nm=nm, c=c: e.activation(bc[nm][:, c * 512:(c + 1) * 512],
                                                               ps_bc[:, :], AF.Identity),
                     reads=["ps_bc"], writes=["bc_" + nm + "_p"])
            p.op("ve", lambda e: e.memset(nopb[:], 0.0), reads=["bc_" + nm + "_p"], writes=["bc_" + nm, "nopb"])
        a_tmp = st0.enter_context(nc.sbuf_tensor("a_tmp", [128, NST], F32))
        for mi in range(2):
            for ec in range(ECH):
                cbase = (mi * ECH + ec) * NST
                p.dma("sp", a_tmp[:], mW["Alog"][mi][ec * 128:(ec + 1) * 128, :], writes=["a_tmp"])
                p.op("act", lambda e, cbase=cbase: e.activation(
                    A_sb[:, cbase:cbase + NST], a_tmp[:], AF.Exp),
                    reads=["a_tmp"], writes=[("A", cbase)])
                p.op("ve", lambda e, cbase=cbase: e.tensor_scalar(
                    A_sb[:, cbase:cbase + NST], A_sb[:, cbase:cbase + NST], -1.0, None, ALU.mult),
                    reads=[("A", cbase)], writes=[("A", cbase)])

        st0.close()
        stA = ExitStack()
        build_phase_a(nc, p, stA, dict(Wq=Wq, WqT=WqT, Wk=Wk, Wv=Wv, Wqa=Wqa, Wka=Wka,
                                       bq=bq, bk=bk, bv=bv, bqa=bqa, bka=bka, be0c=be0c),
                      x_in, x1_h, xn1_h, bc, ident_b, ident_f, tri_b, ones_row_b, ones_f, invt, absrun, dbg_h)
        gm_row = stA.enter_context(nc.sbuf_tensor("gm_row", [1, 128], F32))
        gm_sc = stA.enter_context(nc.sbuf_tensor("gm_sc", [1, 1], F32))
        if _DBG == "absrun":
            p.dma("sp", dbg_h[0:128, 0:1], absrun[:, 0:1], reads=["absrun"],
                  writes=[("dbg_h", 0)], slow=True)
        p.dma("sp", gm_row[0:1, :], absrun[:, 0:1],
              reads=["absrun"], writes=["gm_row"], slow=True)
        if _DBG == "gmrow":
            p.dma("sp", dbg_h[0:1, 0:128], gm_row[0:1, :], reads=["gm_row"], writes=[("dbg_h", 0)])
        p.op("ve", lambda e: e.reduce_max(gm_sc[:], gm_row[:], axis=AX), reads=["gm_row"], writes=["gm_sc"])
        if _DBG == "gmsc":
            p.dma("sp", dbg_h[0:1, 0:1], gm_sc[:], reads=["gm_sc"], writes=[("dbg_h", 0)])
        p.dma("gp", cc_in_h, gm_sc[:], reads=["gm_sc"], writes=["cc_in_h"])
        p.tiny.add("cc_out_h")
        p.op("gp", lambda e: e.collective_compute(
            "AllReduce", ALU.max, replica_groups=[list(range(NCORES))],
            ins=[cc_in_h], outs=[cc_out_h]),
            reads=["cc_in_h"], writes=["cc_out_h"], raw_inc=True)
        p.dma("gp", gm_sc[:], cc_out_h, reads=["cc_out_h"], writes=["gm_sc"])
        p.op("ve", lambda e: e.tensor_scalar(qscale[0:1, 0:1], gm_sc[:], 1e-8, 1.0 / 127.0,
                                             ALU.max, ALU.mult),
             reads=["gm_sc"], writes=["qscale_s"])
        p.op("ve", lambda e: e.reciprocal(qscale[0:1, 1:2], qscale[0:1, 0:1]),
             reads=["qscale_s"], writes=["qscale_i"])
        gb_last = dma_bcast_scalar(p, gmax_bc, "gb", qscale[0:1, 1:2], "qscale_i")
        p.barrier()
        stA.close()

        if _PHASES >= 2:
            stB = ExitStack()
            build_phase_b(nc, p, stB, wffT, xn1_h, x1_h, x2_h, xn_h, xnT_h, bc,
                          ident_b, gmax_bc, gb_last, qscale, alpha_s_bc, dbg_h)
            p.barrier()
            stB.close()

        stAB.close()
        if _PHASES >= 3:
            build_phase_c(nc, p, stack, mW, A_sb, ident_b, ident_f, tri_f, bc,
                          xnT_h, xn_h, x2_h, xc_h, z_h, ybf_h, xcb_h, xm_h, x3_h, xn3T_h)

        if _PHASES >= 4 and _TAP is None:
            build_phase_d(nc, p, stack, WrT, WkkT, WvvT, murc, mukc,
                          xn3T_h, x3_h, z_h, xm_h, out_hbm, ident_f)
        else:
            tap = scratches[_TAP] if _TAP else x1_h
            tname = _TAP if _TAP else "x1_h"
            stT = ExitStack()
            tbuf = stT.enter_context(nc.sbuf_tensor("tap_t", [128, D], F32))
            for i in range(NT):
                p.dma("sp", tbuf[:], tap[i * 128:(i + 1) * 128, :],
                      reads=[(tname, i)], writes=["tap_t"])
                p.dma("sp", out_hbm[i * 128:(i + 1) * 128, :], tbuf[:],
                      reads=["tap_t"], writes=[("out_h", i)])
            stT.close()

        outdeps = [p.bufs[k]["w"] for k in p.bufs
                   if isinstance(k, tuple) and k[0] == "out_h" and p.bufs[k]["w"] is not None]
        p.op("sp", lambda e: e.nop(), deps=outdeps)
        p.emit(stack)
    return nc


def build_phase_a(nc, p, stA, W, x_in, x1_h, xn1_h, bc, ident_b, ident_f, tri_b, ones_row_b,
                  ones_f, invt, absrun, dbg_h):
    w_sb = {}
    for nm in ("Wq", "Wk", "Wv", "Wka", "Wqa2"):
        w_sb[nm] = stA.enter_context(nc.sbuf_tensor("w_" + nm, [128, DCH * D], BF16))
    wld = [stA.enter_context(nc.sbuf_tensor(f"wldA{i}", [128, D], F32)) for i in range(2)]
    for nm in ("Wq", "Wk", "Wv", "Wka"):
        for kc in range(DCH):
            sl = kc % 2
            p.dma("sp", wld[sl][:], W[nm][kc * 128:(kc + 1) * 128, :], writes=[("wldA", sl)])
            p.op("act", lambda e, nm=nm, kc=kc, sl=sl: e.activation(
                w_sb[nm][:, kc * D:(kc + 1) * D], wld[sl][:], AF.Copy),
                reads=[("wldA", sl)], writes=[("w", nm, kc)])
    brow = {nm: stA.enter_context(nc.sbuf_tensor("brow_" + nm, [1, D], BF16))
            for nm in ("q", "k", "v", "sq", "ka")}
    be0_b = stA.enter_context(nc.sbuf_tensor("be0_b", [128, DCH], BF16))
    be0_f = stA.enter_context(nc.sbuf_tensor("be0_f", [128, DCH], F32))
    brow_f = stA.enter_context(nc.sbuf_tensor("brow_f", [1, D], F32))
    bpq_c = stA.enter_context(nc.sbuf_tensor("bpq_c", [128, DCH], BF16))
    stW = ExitStack()
    wqt_b = stW.enter_context(nc.sbuf_tensor("wqt_b", [128, DCH * D], BF16))
    wqa_b = stW.enter_context(nc.sbuf_tensor("wqa_b", [128, DCH * D], BF16))
    for nm, dst in (("WqT", wqt_b), ("Wqa", wqa_b)):
        for kc in range(DCH):
            sl = kc % 2
            p.dma("sp", wld[sl][:], W[nm][kc * 128:(kc + 1) * 128, :], writes=[("wldA", sl)])
            p.op("act", lambda e, dst=dst, kc=kc, sl=sl: e.activation(
                dst[:, kc * D:(kc + 1) * D], wld[sl][:], AF.Copy),
                reads=[("wldA", sl)], writes=[(nm, kc)])
    ps_q = stA.enter_context(nc.psum_tensor("ps_q", [128, 1024], F32))
    ps_s = stA.enter_context(nc.psum_tensor("ps_s", [128, 1024], F32))
    ps_t = stA.enter_context(nc.psum_tensor("ps_t", [128, 1024], BF16))
    ps_c = stA.enter_context(nc.psum_tensor("ps_c", [128, 1024], F32))
    for dc in range(DCH):
        for c in range(2):
            for kc in range(DCH):
                p.op("pe", lambda e, dc=dc, c=c, kc=kc: e.matmul(
                    ps_q[:, c * 512:(c + 1) * 512],
                    wqt_b[:, kc * D + dc * 128: kc * D + (dc + 1) * 128],
                    wqa_b[:, kc * D + c * 512:kc * D + (c + 1) * 512],
                    start=(kc == 0), stop=(kc == DCH - 1)),
                    reads=[("WqT", kc), ("Wqa", kc)], writes=["ps_q"])
        p.op("act", lambda e, dc=dc: e.activation(w_sb["Wqa2"][:, dc * D:(dc + 1) * D],
                                                  ps_q[:, :], AF.Copy),
             reads=["ps_q"], writes=[("w", "Wqa2", dc)])

    p.dma("sp", be0_f[:], W["be0c"].rearrange("(c p) one -> p (c one)", p=128), writes=["be0_f"], slow=True)
    p.op("ve", lambda e: e.tensor_copy(be0_b[:], be0_f[:]), reads=["be0_f"], writes=["be0_b"])

    def bias_row(nm, wname, bsrc):
        p.dma("sp", brow_f[:], bsrc, writes=["brow_f"])
        p.op("ve", lambda e, nm=nm: e.tensor_copy(brow[nm][:], brow_f[:]),
             reads=["brow_f"], writes=[("brow0", nm)])
        for c in range(2):
            p.op("pe", lambda e, nm=nm, c=c: e.matmul(
                ps_s[0:1, c * 512:(c + 1) * 512], ones_row_b[0:1, 0:1],
                brow[nm][0:1, c * 512:(c + 1) * 512], start=True, stop=False),
                reads=[("brow0", nm), "ones_row_b"], writes=["ps_s"])
            for kc in range(DCH):
                p.op("pe", lambda e, wname=wname, kc=kc, c=c: e.matmul(
                    ps_s[0:1, c * 512:(c + 1) * 512], be0_b[:, kc:kc + 1],
                    w_sb[wname][:, kc * D + c * 512: kc * D + (c + 1) * 512],
                    start=False, stop=(kc == DCH - 1)),
                    reads=["be0_b", ("w", wname, kc)], writes=["ps_s"])
        p.op("act", lambda e, nm=nm: e.activation(brow[nm][:], ps_s[0:1, :], AF.Copy),
             reads=["ps_s"], writes=[("brow", nm)])

    bias_row("q", "Wq", W["bq"])
    bias_row("k", "Wk", W["bk"])
    bias_row("v", "Wv", W["bv"])
    bpq_h = nc.dram_tensor("bpq_h", [1, D], BF16).ap()
    p.dma("sp", bpq_h, brow["q"][0:1, :], reads=[("brow", "q")], writes=["bpq_h"])
    p.dma("sp", bpq_c[:], bpq_h.rearrange("one (c p) -> p (one c)", p=128),
          reads=["bpq_h"], writes=["bpq_c"], slow=True)
    p.dma("sp", brow_f[:], W["bqa"], writes=["brow_f"])
    p.op("ve", lambda e: e.tensor_copy(brow["sq"][:], brow_f[:]), reads=["brow_f"], writes=[("brow0", "sq")])
    for c in range(2):
        p.op("pe", lambda e, c=c: e.matmul(
            ps_s[0:1, c * 512:(c + 1) * 512], ones_row_b[0:1, 0:1],
            brow["sq"][0:1, c * 512:(c + 1) * 512], start=True, stop=False),
            reads=[("brow0", "sq"), "ones_row_b"], writes=["ps_s"])
        for kc in range(DCH):
            p.op("pe", lambda e, kc=kc, c=c: e.matmul(
                ps_s[0:1, c * 512:(c + 1) * 512], bpq_c[:, kc:kc + 1],
                wqa_b[:, kc * D + c * 512: kc * D + (c + 1) * 512],
                start=False, stop=(kc == DCH - 1)),
                reads=["bpq_c", ("Wqa", kc)], writes=["ps_s"])
    p.op("act", lambda e: e.activation(brow["sq"][:], ps_s[0:1, :], AF.Copy),
         reads=["ps_s"], writes=[("brow", "sq")])
    p.dma("sp", brow_f[:], W["bka"], writes=["brow_f"])
    p.op("ve", lambda e: e.tensor_copy(brow["ka"][:], brow_f[:]), reads=["brow_f"], writes=[("brow", "ka")])
    p.barrier()
    stW.close()

    tiny = alloc_tiny(nc, stA, ["a0", "a1", "a2", "a3", "a4"], p)
    sm = {nm: stA.enter_context(nc.sbuf_tensor("sm_" + nm, [128, 1], F32))
          for nm in ("nmx", "nmxs", "esum", "rs", "nkx", "nkxs", "ksum", "rk", "am")}
    for nm in sm:
        p.tiny.add("sm_" + nm)
    for k in ("carry_q", "carry_k", "carry_f", "absrun"):
        p.tiny.add(k)
    T = {}
    for nm in ("scrt", "ln0", "hgt", "qkf", "vf", "qwkw", "cqck", "pqpk", "mixedf"):
        T[nm] = stA.enter_context(nc.sbuf_tensor(nm, [128, D], F32))
    for nm in ("hg", "hgT", "ppre", "kpre", "mixT"):
        T[nm] = stA.enter_context(nc.sbuf_tensor(nm, [128, D], BF16))
    xt = [stA.enter_context(nc.sbuf_tensor(f"xt{i}", [128, D], F32)) for i in range(2)]
    carry_q = stA.enter_context(nc.sbuf_tensor("carry_q", [1, D], BF16))
    carry_k = stA.enter_context(nc.sbuf_tensor("carry_k", [1, D], BF16))
    carry_f = stA.enter_context(nc.sbuf_tensor("carry_f", [1, D], F32))

    def mm_chain(psname, ps_ap, lhsT, lhsT_key, w_name, bias_name):
        for c in range(2):
            p.op("pe", lambda e, c=c, bias_name=bias_name: e.matmul(
                ps_ap[:, c * 512:(c + 1) * 512], ones_row_b[0:1, :],
                brow[bias_name][0:1, c * 512:(c + 1) * 512], start=True, stop=False),
                reads=[("brow", bias_name), "ones_row_b"], writes=[psname])
            for kc in range(DCH):
                p.op("pe", lambda e, c=c, kc=kc, w_name=w_name, lhsT=lhsT: e.matmul(
                    ps_ap[:, c * 512:(c + 1) * 512], lhsT[:, kc * 128:(kc + 1) * 128],
                    w_sb[w_name][:, kc * D + c * 512:kc * D + (c + 1) * 512],
                    start=False, stop=(kc == DCH - 1)),
                    reads=[lhsT_key, ("w", w_name, kc)], writes=[psname])

    def transpose8(src, src_key, dst, dst_key, ident=None, ident_key="ident_b", ps=None, ps_key="ps_t"):
        idt = ident if ident is not None else ident_b
        pst = ps if ps is not None else ps_t
        for j in range(DCH):
            p.op("pe", lambda e, j=j, idt=idt, pst=pst: e.transpose(
                pst[:, j * 128:(j + 1) * 128], src[:, j * 128:(j + 1) * 128], idt[:]),
                 reads=[src_key, ident_key], writes=[ps_key])
        p.op("act", lambda e, pst=pst: e.activation(dst[:], pst[:, 0:1024], AF.Copy),
             reads=[ps_key], writes=[dst_key])

    def softmax(psname, ps_ap, outbuf, outkey, pn, px, ps_, pr):
        p.op("ve", lambda e: e.reduce_max(sm[pn][:], ps_ap[:, :], axis=AX, negate=True),
             reads=[psname], writes=["sm_" + pn])
        p.op("act", lambda e: e.activation(sm[px][:], sm[pn][:], AF.Copy, scale=1.0 / 32.0),
             reads=["sm_" + pn], writes=["sm_" + px])
        p.op("act", lambda e: e.activation(outbuf[:], ps_ap[:, :], AF.Exp, scale=1.0 / 32.0,
                                           bias=sm[px][:], accum_out=sm[ps_][:]),
             reads=[psname], hard=["sm_" + px], writes=[outkey, "sm_" + ps_])
        p.op("ve", lambda e: e.reciprocal(sm[pr][:], sm[ps_][:]),
             reads=["sm_" + ps_], writes=["sm_" + pr])

    for i in range(NT):
        j = i % NTB
        xti = xt[i % 2]
        xkey = ("xt", i % 2)

        def dbg(nm, ap, key, i=i):
            if _DBG == nm:
                p.dma("sp", dbg_h[i * 128:(i + 1) * 128, :], ap, reads=[key],
                      writes=[("dbg_h", i)])

        def dbgc(nm, ap, key, i=i):
            if _DBG == nm:
                p.dma("sp", dbg_h[i * 128:(i + 1) * 128, 0:1], ap, reads=[key],
                      writes=[("dbg_h", i)], slow=True)

        p.dma("sp", xti[:], x_in[i * 128:(i + 1) * 128, :], writes=[xkey])
        r0, n0, h0 = ln_stats(p, xti[:], xkey, T["scrt"][:], "scrt", tiny, "a0", D)
        dbgc("a0rstd", r0[:], "a0_rstd")
        dbgc("a0nmr", n0[:], "a0_nmr")
        dbgc("a0mu", tiny["a0_mu"][:], "a0_mu")
        dbgc("a0sumsq", tiny["a0_sumsq"][:], "a0_sumsq")
        ln_apply(p, T["ln0"][:], "ln0", xti[:], xkey, r0, n0, h0)
        dbg("ln0", T["ln0"][:], "ln0")
        r1, n1, h1 = ln_stats(p, T["ln0"][:], "ln0", T["scrt"][:], "scrt", tiny, "a1", D)
        ln_apply(p, T["hgt"][:], "hgt", T["ln0"][:], "ln0", r1, n1, h1)
        p.op("ve", lambda e: e.tensor_mul(T["hg"][:], T["hgt"][:], bc["g0"][:]),
             reads=["hgt", "bc_g0"], writes=["hg"])
        dbg("hgt", T["hgt"][:], "hgt")
        transpose8(T["hg"], "hg", T["hgT"], "hgT")
        mm_chain("ps_q", ps_q, T["hgT"], "hgT", "Wq", "q")
        p.op("ve", lambda e: e.tensor_copy(T["qkf"][:], ps_q[:, :]), reads=["ps_q"], writes=["qkf"])
        dbg("qf", T["qkf"][:], "qkf")
        mm_chain("ps_s", ps_s, T["hgT"], "hgT", "Wqa2", "sq")
        softmax("ps_s", ps_s, T["qwkw"], "qwkw", "nmx", "nmxs", "esum", "rs")
        dbg("qwe", T["qwkw"][:], "qwkw")
        p.op("ve", lambda e: e.tensor_scalar(T["qwkw"][:], T["qwkw"][:], sm["rs"][:], None, ALU.mult),
             reads=["qwkw"], hard=["sm_rs"], writes=["qwkw"])
        p.op("ve", lambda e: e.tensor_mul(T["ppre"][:], T["qwkw"][:], T["qkf"][:]),
             reads=["qwkw", "qkf"], writes=["ppre"])
        dbg("ppre", T["ppre"][:], "ppre")
        if j == 0:
            p.op("gp", lambda e: e.memset(carry_q[:], 0.0), writes=["carry_q"])
            p.op("gp", lambda e: e.memset(carry_k[:], 0.0), writes=["carry_k"])
        for c in range(2):
            p.op("pe", lambda e, c=c: e.matmul(ps_c[:, c * 512:(c + 1) * 512], ones_row_b[0:1, :],
                                               carry_q[0:1, c * 512:(c + 1) * 512],
                                               start=True, stop=False),
                 reads=["carry_q", "ones_row_b"], writes=["ps_c"])
            p.op("pe", lambda e, c=c: e.matmul(ps_c[:, c * 512:(c + 1) * 512], tri_b[:],
                                               T["ppre"][:, c * 512:(c + 1) * 512],
                                               start=False, stop=True),
                 reads=["tri_b", "ppre"], writes=["ps_c"])
        p.op("ve", lambda e: e.tensor_copy(T["cqck"][:], ps_c[:, :]), reads=["ps_c"], writes=["cqck"])
        p.dma("sp", carry_f[:], T["cqck"][127:128, :], reads=["cqck"], writes=["carry_f"])
        p.op("act", lambda e: e.activation(carry_q[:], carry_f[:], AF.Copy),
             reads=["carry_f"], writes=["carry_q"])
        p.op("ve", lambda e, j=j: e.tensor_scalar(T["pqpk"][:], T["cqck"][:], invt[:, j:j + 1],
                                                  None, ALU.mult),
             reads=["cqck"], hard=["invt"], writes=["pqpk"])
        dbg("pq", T["pqpk"][:], "pqpk")
        mm_chain("ps_q", ps_q, T["hgT"], "hgT", "Wk", "k")
        p.op("ve", lambda e: e.tensor_copy(T["qkf"][:], ps_q[:, :]), reads=["ps_q"], writes=["qkf"])
        mm_chain("ps_s", ps_s, T["hgT"], "hgT", "Wv", "v")
        p.op("ve", lambda e: e.tensor_copy(T["vf"][:], ps_s[:, :]), reads=["ps_s"], writes=["vf"])
        p.op("ve", lambda e: e.tensor_mul(T["mixedf"][:], T["pqpk"][:], T["qkf"][:]),
             reads=["pqpk", "qkf"], writes=["mixedf"])
        dbg("mixed", T["mixedf"][:], "mixedf")
        transpose8(T["mixedf"], "mixedf", T["mixT"], "mixT", ident=ident_f, ident_key="ident_f",
                   ps=ps_c, ps_key="ps_c")
        mm_chain("ps_q", ps_q, T["mixT"], "mixT", "Wka", "ka")
        softmax("ps_q", ps_q, T["qwkw"], "qwkw", "nkx", "nkxs", "ksum", "rk")
        p.op("ve", lambda e: e.tensor_scalar(T["qwkw"][:], T["qwkw"][:], sm["rk"][:], None, ALU.mult),
             reads=["qwkw"], hard=["sm_rk"], writes=["qwkw"])
        p.op("ve", lambda e: e.tensor_mul(T["kpre"][:], T["qwkw"][:], T["mixedf"][:]),
             reads=["qwkw", "mixedf"], writes=["kpre"])
        for c in range(2):
            p.op("pe", lambda e, c=c: e.matmul(ps_c[:, c * 512:(c + 1) * 512], ones_row_b[0:1, :],
                                               carry_k[0:1, c * 512:(c + 1) * 512],
                                               start=True, stop=False),
                 reads=["carry_k", "ones_row_b"], writes=["ps_c"])
            p.op("pe", lambda e, c=c: e.matmul(ps_c[:, c * 512:(c + 1) * 512], tri_b[:],
                                               T["kpre"][:, c * 512:(c + 1) * 512],
                                               start=False, stop=True),
                 reads=["tri_b", "kpre"], writes=["ps_c"])
        p.op("ve", lambda e: e.tensor_copy(T["cqck"][:], ps_c[:, :]), reads=["ps_c"], writes=["cqck"])
        p.dma("sp", carry_f[:], T["cqck"][127:128, :], reads=["cqck"], writes=["carry_f"])
        p.op("act", lambda e: e.activation(carry_k[:], carry_f[:], AF.Copy),
             reads=["carry_f"], writes=["carry_k"])
        p.op("ve", lambda e, j=j: e.tensor_scalar(T["pqpk"][:], T["cqck"][:], invt[:, j:j + 1],
                                                  None, ALU.mult),
             reads=["cqck"], hard=["invt"], writes=["pqpk"])
        p.op("ve", lambda e: e.tensor_mul(T["qwkw"][:], T["pqpk"][:], T["vf"][:]),
             reads=["pqpk", "vf"], writes=["qwkw"])
        dbg("attn", T["qwkw"][:], "qwkw")
        r2, n2, h2 = ln_stats(p, T["qwkw"][:], "qwkw", T["scrt"][:], "scrt", tiny, "a2", D)
        ln_apply(p, T["mixedf"][:], "mixedf", T["qwkw"][:], "qwkw", r2, n2, h2)
        p.op("ve", lambda e, xti=xti: e.tensor_add(T["vf"][:], xti[:], T["mixedf"][:]),
             reads=[xkey, "mixedf"], writes=["vf"])
        p.dma("sp", x1_h[i * 128:(i + 1) * 128, :], T["vf"][:], reads=["vf"], writes=[("x1_h", i)])
        r3, n3, h3 = ln_stats(p, T["vf"][:], "vf", T["scrt"][:], "scrt", tiny, "a3", D)
        ln_apply(p, T["ln0"][:], "ln0", T["vf"][:], "vf", r3, n3, h3)
        r4, n4, h4 = ln_stats(p, T["ln0"][:], "ln0", T["scrt"][:], "scrt", tiny, "a4", D)
        ln_apply(p, T["hgt"][:], "hgt", T["ln0"][:], "ln0", r4, n4, h4)
        p.op("ve", lambda e: e.tensor_mul(T["hgt"][:], T["hgt"][:], bc["gff"][:]),
             reads=["hgt", "bc_gff"], writes=["hgt"])
        p.op("ve", lambda e: e.tensor_add(T["hgt"][:], T["hgt"][:], bc["bff"][:]),
             reads=["hgt", "bc_bff"], writes=["hgt"])
        p.op("ve", lambda e: e.reduce_max(sm["am"][:], T["hgt"][:], axis=AX,
                                          apply_absolute_value=True),
             reads=["hgt"], writes=["sm_am"])
        p.op("ve", lambda e: e.tensor_max(absrun[:], absrun[:], sm["am"][:]),
             reads=["absrun", "sm_am"], writes=["absrun"])
        p.dma("sp", xn1_h[i * 128:(i + 1) * 128, :], T["hgt"][:], reads=["hgt"], writes=[("xn1_h", i)])


def build_phase_b(nc, p, stB, wffT, xn1_h, x1_h, x2_h, xn_h, xnT_h, bc,
                  ident_b, gmax_bc, gb_last, qscale, alpha_s_bc, dbg_h):
    wt_b = stB.enter_context(nc.sbuf_tensor("wt_b", [128, DCH * D], BF16))
    wldB = [stB.enter_context(nc.sbuf_tensor(f"wldB{i}", [128, D], F32)) for i in range(2)]
    wabs = stB.enter_context(nc.sbuf_tensor("wabs", [128, D], F32))
    wsgn = stB.enter_context(nc.sbuf_tensor("wsgn", [128, D], F32))
    wmask = stB.enter_context(nc.sbuf_tensor("wmask", [128, D], F32))
    wsum = stB.enter_context(nc.sbuf_tensor("wsum", [128, 1], F32))
    wacc = stB.enter_context(nc.sbuf_tensor("wacc", [128, 4], F32))
    wrow = stB.enter_context(nc.sbuf_tensor("wrow", [1, 128], F32))
    wsc = stB.enter_context(nc.sbuf_tensor("wsc", [1, 4], F32))
    delta_bc = stB.enter_context(nc.sbuf_tensor("delta_bc", [128, 1], F32))
    ps_f = stB.enter_context(nc.psum_tensor("ps_f", [128, 1024], F32))
    ps_tB = stB.enter_context(nc.psum_tensor("ps_tB", [128, 1024], BF16))
    tinyB = alloc_tiny(nc, stB, ["b0", "b1", "b2"], p)

    p.op("gp", lambda e: e.memset(wacc[:], 0.0), writes=["wacc"])
    for kc in range(DCH):
        sl = kc % 2
        p.dma("sp", wldB[sl][:], wffT[kc * 128:(kc + 1) * 128, :], writes=[("wldB", sl)])
        p.op("act", lambda e, sl=sl: e.activation(wabs[:], wldB[sl][:], AF.Abs),
             reads=[("wldB", sl)], writes=["wabs"])
        p.op("ve", lambda e: e.reduce_sum(wsum[:], wabs[:], axis=AX), reads=["wabs"], writes=["wsum"])
        p.op("ve", lambda e: e.tensor_add(wacc[:, 0:1], wacc[:, 0:1], wsum[:]),
             reads=["wacc", "wsum"], writes=["wacc"])
    p.dma("sp", wrow[0:1, :], wacc[:, 0:1],
          reads=["wacc"], writes=["wrow"], slow=True)
    p.op("ve", lambda e: e.reduce_sum(wsc[0:1, 0:1], wrow[:], axis=AX), reads=["wrow"], writes=["wsc0"])
    p.op("ve", lambda e: e.tensor_scalar(wsc[0:1, 1:2], wsc[0:1, 0:1], 0.7 / (D * D), None, ALU.mult),
         reads=["wsc0"], writes=["wsc_d"])
    db_last = dma_bcast_scalar(p, delta_bc, "db", wsc[0:1, 1:2], "wsc_d")
    p.op("gp", lambda e: e.memset(wacc[:], 0.0), reads=["wacc"], writes=["wacc"])
    for kc in range(DCH):
        sl = kc % 2
        p.dma("sp", wldB[sl][:], wffT[kc * 128:(kc + 1) * 128, :], writes=[("wldB", sl)])
        p.op("act", lambda e, sl=sl: e.activation(wabs[:], wldB[sl][:], AF.Abs),
             reads=[("wldB", sl)], writes=["wabs"])
        p.op("act", lambda e, sl=sl: e.activation(wsgn[:], wldB[sl][:], AF.Sign),
             reads=[("wldB", sl)], writes=["wsgn"])
        p.op("ve", lambda e: e.tensor_scalar(wmask[:], wabs[:], delta_bc[:], None, ALU.is_gt),
             reads=["wabs"], hard=[db_last], writes=["wmask"])
        p.op("ve", lambda e, kc=kc: e.tensor_mul(wt_b[:, kc * D:(kc + 1) * D], wsgn[:], wmask[:]),
             reads=["wsgn", "wmask"], writes=[("wt", kc)])
        p.op("ve", lambda e: e.tensor_mul(wabs[:], wabs[:], wmask[:]),
             reads=["wabs", "wmask"], writes=["wabs"])
        p.op("ve", lambda e: e.reduce_sum(wsum[:], wabs[:], axis=AX), reads=["wabs"], writes=["wsum"])
        p.op("ve", lambda e: e.tensor_add(wacc[:, 1:2], wacc[:, 1:2], wsum[:]),
             reads=["wacc", "wsum"], writes=["wacc"])
        p.op("ve", lambda e: e.reduce_sum(wsum[:], wmask[:], axis=AX), reads=["wmask"], writes=["wsum"])
        p.op("ve", lambda e: e.tensor_add(wacc[:, 2:3], wacc[:, 2:3], wsum[:]),
             reads=["wacc", "wsum"], writes=["wacc"])
    p.dma("sp", wrow[0:1, :], wacc[:, 1:2],
          reads=["wacc"], writes=["wrow"], slow=True)
    p.op("ve", lambda e: e.reduce_sum(wsc[0:1, 2:3], wrow[:], axis=AX), reads=["wrow"], writes=["wsc_ma"])
    p.dma("sp", wrow[0:1, :], wacc[:, 2:3],
          reads=["wacc", "wsc_ma"], writes=["wrow"], slow=True)
    p.op("ve", lambda e: e.reduce_sum(wsc[0:1, 3:4], wrow[:], axis=AX), reads=["wrow"], writes=["wsc_m"])
    p.op("ve", lambda e: e.tensor_scalar(wsc[0:1, 3:4], wsc[0:1, 3:4], 1.0, None, ALU.max),
         reads=["wsc_m"], writes=["wsc_m"])
    p.op("ve", lambda e: e.reciprocal(wsc[0:1, 0:1], wsc[0:1, 3:4]), reads=["wsc_m"], writes=["wsc_r"])
    p.op("ve", lambda e: e.tensor_mul(wsc[0:1, 1:2], wsc[0:1, 2:3], wsc[0:1, 0:1]),
         reads=["wsc_ma", "wsc_r"], writes=["wsc_alpha"])
    p.op("ve", lambda e: e.tensor_mul(wsc[0:1, 1:2], wsc[0:1, 1:2], qscale[0:1, 0:1]),
         reads=["wsc_alpha", "qscale_s"], writes=["wsc_as"])
    ab_last = dma_bcast_scalar(p, alpha_s_bc, "ab", wsc[0:1, 1:2], "wsc_as")

    TB = {}
    for nm in ("scrB", "ffo", "lnf", "x1r", "x2t", "gt", "xnm"):
        TB[nm] = stB.enter_context(nc.sbuf_tensor(nm, [128, D], F32))
    xq32 = stB.enter_context(nc.sbuf_tensor("xq32", [128, D], I32))
    xqb = stB.enter_context(nc.sbuf_tensor("xqb", [128, D], BF16))
    xqT = stB.enter_context(nc.sbuf_tensor("xqT", [128, D], BF16))
    xnt = [stB.enter_context(nc.sbuf_tensor(f"xnt{i}", [128, D], F32)) for i in range(2)]

    for i in range(NT):
        sl = i % 2

        def dbg(nm, ap, key, i=i, slow=False):
            if _DBG == nm:
                p.dma("sp", dbg_h[i * 128:(i + 1) * 128, 0:1] if slow else dbg_h[i * 128:(i + 1) * 128, :],
                      ap, reads=[key], writes=[("dbg_h", i)], slow=slow)

        p.dma("sp", xnt[sl][:], xn1_h[i * 128:(i + 1) * 128, :],
              reads=[("xn1_h", i)], writes=[("xnt", sl)])
        p.op("ve", lambda e, sl=sl: e.tensor_scalar(TB["scrB"][:], xnt[sl][:], gmax_bc[:],
                                                    None, ALU.mult),
             reads=[("xnt", sl)], hard=[gb_last], writes=["scrB"])
        p.op("ve", lambda e: e.tensor_scalar(TB["scrB"][:], TB["scrB"][:], 127.0, -127.0,
                                             ALU.min, ALU.max),
             reads=["scrB"], writes=["scrB"])
        p.op("ve", lambda e: e.tensor_copy(xq32[:], TB["scrB"][:]), reads=["scrB"], writes=["xq32"])
        p.op("ve", lambda e: e.tensor_copy(xqb[:], xq32[:]), reads=["xq32"], writes=["xqb"])
        if _DBG == "xq":
            p.op("ve", lambda e: e.tensor_copy(TB["scrB"][:], xq32[:]), reads=["xq32"], writes=["scrB"])
            dbg("xq", TB["scrB"][:], "scrB")
        for jj in range(DCH):
            p.op("pe", lambda e, jj=jj: e.transpose(ps_tB[:, jj * 128:(jj + 1) * 128],
                                                    xqb[:, jj * 128:(jj + 1) * 128], ident_b[:]),
                 reads=["xqb", "ident_b"], writes=["ps_tB"])
        p.op("act", lambda e: e.activation(xqT[:], ps_tB[:, :], AF.Copy), reads=["ps_tB"], writes=["xqT"])
        for c in range(2):
            for kc in range(DCH):
                p.op("pe", lambda e, c=c, kc=kc: e.matmul(
                    ps_f[:, c * 512:(c + 1) * 512], xqT[:, kc * 128:(kc + 1) * 128],
                    wt_b[:, kc * D + c * 512:kc * D + (c + 1) * 512],
                    start=(kc == 0), stop=(kc == DCH - 1)),
                    reads=["xqT", ("wt", kc)], writes=["ps_f"])
        p.op("act", lambda e: e.activation(TB["ffo"][:], ps_f[:, :], AF.Identity, scale=alpha_s_bc[:]),
             reads=["ps_f"], hard=[ab_last], writes=["ffo"])
        dbg("ffo", TB["ffo"][:], "ffo")
        dbg("qsc", gmax_bc[:], "qscale_i", slow=True)
        dbg("asc", alpha_s_bc[:], "wsc_as", slow=True)
        r0, n0, h0 = ln_stats(p, TB["ffo"][:], "ffo", TB["scrB"][:], "scrB", tinyB, "b0", D)
        ln_apply(p, TB["lnf"][:], "lnf", TB["ffo"][:], "ffo", r0, n0, h0)
        p.dma("sp", TB["x1r"][:], x1_h[i * 128:(i + 1) * 128, :], reads=[("x1_h", i)], writes=["x1r"])
        p.op("ve", lambda e: e.tensor_add(TB["x2t"][:], TB["x1r"][:], TB["lnf"][:]),
             reads=["x1r", "lnf"], writes=["x2t"])
        p.dma("sp", x2_h[i * 128:(i + 1) * 128, :], TB["x2t"][:], reads=["x2t"], writes=[("x2_h", i)])
        r1, n1, h1 = ln_stats(p, TB["x2t"][:], "x2t", TB["scrB"][:], "scrB", tinyB, "b1", D)
        ln_apply(p, TB["gt"][:], "gt", TB["x2t"][:], "x2t", r1, n1, h1)
        r2, n2, h2 = ln_stats(p, TB["gt"][:], "gt", TB["scrB"][:], "scrB", tinyB, "b2", D)
        ln_apply(p, TB["xnm"][:], "xnm", TB["gt"][:], "gt", r2, n2, h2)
        p.op("ve", lambda e: e.tensor_mul(TB["xnm"][:], TB["xnm"][:], bc["ln1w"][:]),
             reads=["xnm", "bc_ln1w"], writes=["xnm"])
        p.op("ve", lambda e: e.tensor_add(TB["xnm"][:], TB["xnm"][:], bc["ln1b"][:]),
             reads=["xnm", "bc_ln1b"], writes=["xnm"])
        p.dma("sp", xn_h[i * 128:(i + 1) * 128, :], TB["xnm"][:], reads=["xnm"], writes=[("xn_h", i)])
        p.op("ve", lambda e: e.tensor_copy(xqb[:], TB["xnm"][:]), reads=["xnm"], writes=["xqb"])
        for jj in range(DCH):
            p.op("pe", lambda e, jj=jj: e.transpose(ps_tB[:, jj * 128:(jj + 1) * 128],
                                                    xqb[:, jj * 128:(jj + 1) * 128], ident_b[:]),
                 reads=["xqb", "ident_b"], writes=["ps_tB"])
        p.op("act", lambda e: e.activation(xqT[:], ps_tB[:, :], AF.Copy), reads=["ps_tB"], writes=["xqT"])
        bb = i // NTB
        tt = (i % NTB) * 128
        for jj in range(DCH):
            p.dma("sp", xnT_h[bb, jj * 128:(jj + 1) * 128, tt:tt + 128],
                  xqT[:, jj * 128:(jj + 1) * 128],
                  reads=["xqT"], writes=[("xnT_h", i, jj)])


def build_phase_c(nc, p, stack, mW, A_sb, ident_b, ident_f, tri_f, bc,
                  xnT_h, xn_h, x2_h, xc_h, z_h, ybf_h, xcb_h, xm_h, x3_h, xn3T_h):
    stC = ExitStack()
    sel_b = stC.enter_context(nc.sbuf_tensor("sel_b", [128, NST * 128], BF16))
    stSel = ExitStack()
    onesel = stSel.enter_context(nc.sbuf_tensor("onesel", [128, NST * 128], BF16))
    p.op("gp", lambda e: e.memset(onesel[:], 1.0), writes=["onesel"])
    p.op("gp", lambda e: e.affine_select(
        sel_b[:].rearrange("p (n j) -> p n j", n=NST), onesel[:].rearrange("p (n j) -> p n j", n=NST),
        [[-1, NST], [0, 128]], ALU.is_equal, 0.0, base=-64, channel_multiplier=1),
        reads=["onesel"], writes=["sel_b"])
    p.barrier()
    stSel.close()
    ps_w = stC.enter_context(nc.psum_tensor("ps_cw", [128, 1024], F32))
    ps_b2 = stC.enter_context(nc.psum_tensor("ps_cb", [128, 1024], F32))
    ps_y = stC.enter_context(nc.psum_tensor("ps_cy", [128, 2048], F32))
    xnT_sb = stC.enter_context(nc.sbuf_tensor("c_xnT", [128, DCH * L], BF16))
    global _DLB, _CC, _WXP
    _DLB = stC.enter_context(nc.sbuf_tensor("c_dlBg", [128, L], BF16))
    _CC = stC.enter_context(nc.sbuf_tensor("c_Cg", [128, L], BF16))
    _WXP = stC.enter_context(nc.sbuf_tensor("c_wxpg", [128, ECH * 192], BF16))
    p.op("gp", lambda e: e.memset(_CC[0:64, :], 0.0), writes=["c_C"])

    for b in range(BL):
        for kc in range(DCH):
            p.dma("sp", xnT_sb[:, kc * L:(kc + 1) * L], xnT_h[b, kc * 128:(kc + 1) * 128, :],
                  writes=[("cxnT", kc)])
        for dr in range(2):
            _c_stage1(nc, p, mW, dr, (dr == 0), xnT_sb, ps_w, xc_h, z_h, xcb_h)
            _c_stage2(nc, p, mW, dr, (dr == 0), A_sb, sel_b, ident_b,
                      ps_w, ps_b2, ps_y, xc_h, z_h, ybf_h)
            _c_stage3(nc, p, mW, dr, ps_w, ps_b2, ps_y, ybf_h, xm_h)
        _c_stage4(nc, p, b, bc, ident_f, ps_y, xm_h, xn_h, x2_h, x3_h, xn3T_h)
    stC.close()


def _c_stage1(nc, p, mW, dr, fwd, xnT_sb, ps_w, xc_h, z_h, xcb_h):
    st1 = ExitStack()
    xcb1 = _sbu(nc, st1, "c_xcb1", [128, L], BF16)
    xcs = [_sbu(nc, st1, f"c_xcs{i}", [128, 512], BF16) for i in range(2)]
    wch_f = _sbu(nc, st1, "c_wchf", [128, 1024], F32)
    wch_b = _sbu(nc, st1, "c_wchb", [128, 1024], BF16)
    xp_t = _sbu(nc, st1, "c_xp", [128, L + 3], F32)
    cacc = [_sbu(nc, st1, f"c_cacc{i}", [128, L], F32) for i in range(2)]
    convw_t = _sbu(nc, st1, "c_convw", [128, 4], F32)
    convb_t = _sbu(nc, st1, "c_convb", [128, 1], F32)
    xcf1 = _sbu(nc, st1, "c_xcf1", [128, L], F32)
    zf1 = _sbu(nc, st1, "c_zf1", [128, L], F32)
    pad_off = 3 if fwd else 0
    if fwd:
        p.op("gp", lambda e: e.memset(xp_t[:, 0:3], 0.0), writes=["xp_pad"])
    else:
        p.op("gp", lambda e: e.memset(xp_t[:, L:L + 3], 0.0), writes=["xp_pad"])
    for ec in range(ECH):
        p.dma("sp", convw_t[:], mW["convw"][dr][ec * 128:(ec + 1) * 128, :], writes=["convw"])
        p.dma("sp", convb_t[:], mW["convb"][dr][ec * 128:(ec + 1) * 128, :], writes=["convb"])
        for half in range(2):
            jbase = half * E + ec * 128
            p.dma("gp", wch_f[:].rearrange("p (kc j) -> p kc j", kc=DCH),
                  mW["WinT"][dr][0:D, jbase:jbase + 128].rearrange("(kc p) j -> p kc j", p=128),
                  writes=["cwf"])
            p.op("act", lambda e: e.activation(wch_b[:], wch_f[:], AF.Copy),
                 reads=["cwf"], writes=["cwb"])
            for tc in range(4):
                def mmfn(e, kc=None, tc=tc):
                    pass
                for kc in range(DCH):
                    p.op("pe", lambda e, kc=kc, tc=tc: e.matmul(
                        ps_w[:, (tc % 2) * 512:((tc % 2) + 1) * 512],
                        wch_b[:, kc * 128:(kc + 1) * 128],
                        xnT_sb[:, kc * L + tc * 512:kc * L + (tc + 1) * 512],
                        start=(kc == 0), stop=(kc == DCH - 1)),
                        reads=["cwb", ("cxnT", kc)], writes=["ps_cwALL"])
                if half == 0:
                    p.op("act", lambda e, tc=tc: e.activation(
                        xp_t[:, pad_off + tc * 512: pad_off + (tc + 1) * 512],
                        ps_w[:, (tc % 2) * 512:((tc % 2) + 1) * 512], AF.Copy),
                        reads=["ps_cwALL"], writes=["c_xp"])
                else:
                    p.op("act", lambda e, tc=tc: e.activation(
                        zf1[:, tc * 512:(tc + 1) * 512],
                        ps_w[:, (tc % 2) * 512:((tc % 2) + 1) * 512], AF.Copy),
                        reads=["ps_cwALL"], writes=["c_zf1"])
            if half == 1:
                p.dma("sp", z_h[ec * 128:(ec + 1) * 128, :], zf1[:],
                      reads=["c_zf1"], writes=[("z_h", ec)])
        offs = [k if fwd else 3 - k for k in range(4)]
        p.op("ve", lambda e, o=offs[0]: e.tensor_scalar(
            cacc[0][:], xp_t[:, o:o + L], convw_t[:, 0:1], None, ALU.mult),
            reads=["c_xp", "xp_pad"], hard=["convw"], writes=["cacc0"])
        for k in (1, 2, 3):
            src, dst = ("cacc0", "cacc1") if k % 2 == 1 else ("cacc1", "cacc0")
            ss, dd = (cacc[0], cacc[1]) if k % 2 == 1 else (cacc[1], cacc[0])
            p.op("ve", lambda e, o=offs[k], k=k, ss=ss, dd=dd: e.scalar_tensor_tensor(
                dd[:], xp_t[:, o:o + L], convw_t[:, k:k + 1], ss[:], ALU.mult, ALU.add),
                reads=["c_xp", src], hard=["convw"], writes=[dst])
        p.op("act", lambda e: e.activation(xcf1[:], cacc[1][:], AF.Silu, bias=convb_t[:]),
             reads=["cacc1"], hard=["convb"], writes=["c_xcf1"])
        p.dma("sp", xc_h[ec * 128:(ec + 1) * 128, :], xcf1[:],
              reads=["c_xcf1"], writes=[("xc_h", ec)])
        p.op("ve", lambda e: e.tensor_copy(xcb1[:], xcf1[:]),
             reads=["c_xcf1"], writes=["c_xcb1"])
        p.dma("sp", xcb_h[ec * 128:(ec + 1) * 128, :], xcb1[:],
              reads=["c_xcb1"], writes=[("xcb_h", ec)])
    global _DLB, _CC
    for ec in range(ECH):
        p.dma("sp", wch_f[:, 0:192], mW["WxpT"][dr][ec * 128:(ec + 1) * 128, :], writes=["cwf"])
        p.op("act", lambda e, ec=ec: e.activation(_WXP[:, ec * 192:(ec + 1) * 192],
                                                  wch_f[:, 0:192], AF.Copy),
             reads=["cwf"], writes=[("wxp", ec)])
    for tc in range(4):
        for ec in range(ECH):
            sl = ec % 2
            p.dma("sp", xcs[sl][:], xcb_h[ec * 128:(ec + 1) * 128, tc * 512:(tc + 1) * 512],
                  reads=[("xcb_h", ec)], writes=[("xcs", sl)])
            p.op("pe", lambda e, tc=tc, ec=ec, sl=sl: e.matmul(
                ps_w[:, 0:512], _WXP[:, ec * 192:ec * 192 + 128],
                xcs[sl][:], start=(ec == 0), stop=(ec == ECH - 1), skip_group_check=True),
                reads=[("wxp", ec), ("xcs", sl)], writes=["ps_cwALL"])
            p.op("pe", lambda e, tc=tc, ec=ec, sl=sl: e.matmul(
                ps_w[0:64, 512:1024], _WXP[:, ec * 192 + 128:ec * 192 + 192],
                xcs[sl][:], start=(ec == 0), stop=(ec == ECH - 1), skip_group_check=True),
                reads=[("wxp", ec), ("xcs", sl)], writes=["ps_cwALL2"])
        p.op("act", lambda e, tc=tc: e.activation(_DLB[:, tc * 512:(tc + 1) * 512],
                                                  ps_w[:, 0:512], AF.Copy),
             reads=["ps_cwALL"], writes=["c_dlB"])
        p.op("act", lambda e, tc=tc: e.activation(_CC[64:128, tc * 512:(tc + 1) * 512],
                                                  ps_w[0:64, 512:1024], AF.Copy),
             reads=["ps_cwALL2"], writes=["c_C"])
    p.barrier()
    st1.close()


def _c_stage2(nc, p, mW, dr, fwd, A_sb, sel_b, ident_b, ps_w, ps_b2, ps_y,
              xc_h, z_h, ybf_h):
    st2 = ExitStack()
    wdt_f = _sbu(nc, st2, "c_wdtf", [64, E], F32)
    wdt_b = _sbu(nc, st2, "c_wdtb", [64, E], BF16)
    dtb_t = _sbu(nc, st2, "c_dtb", [128, 1], F32)
    dp_t = _sbu(nc, st2, "c_dp", [128, 1], F32)
    dt_t = _sbu(nc, st2, "c_dt", [128, L], F32)
    u_t = _sbu(nc, st2, "c_u", [128, L], F32)
    xcf = _sbu(nc, st2, "c_xcf", [128, L], F32)
    zf = _sbu(nc, st2, "c_zf", [128, L], F32)
    y2 = _sbu(nc, st2, "c_y2", [128, L], F32)
    yf = _sbu(nc, st2, "c_yf", [128, L], F32)
    ybt = _sbu(nc, st2, "c_ybt", [128, L], BF16)
    dA = [_sbu(nc, st2, f"c_dA{i}", [128, L], F32) for i in range(2)]
    add_t = _sbu(nc, st2, "c_add", [128, L], F32)
    h_t = _sbu(nc, st2, "c_h", [128, L], F32)
    tmp_t = [_sbu(nc, st2, f"c_tmp{i}", [128, L], BF16) for i in range(2)]
    spz = _sbu(nc, st2, "c_spz", [128, 512], F32)
    spt = _sbu(nc, st2, "c_spt", [128, 512], F32)
    spr = _sbu(nc, st2, "c_spr", [128, 512], F32)
    p.dma("sp", wdt_f[:], mW["WdtT"][dr], writes=["wdtf"])
    p.op("act", lambda e: e.activation(wdt_b[:], wdt_f[:], AF.Copy), reads=["wdtf"], writes=["wdtb"])
    for ec in range(ECH):
        _c_scan_ec(nc, p, mW, dr, fwd, ec, A_sb, sel_b, ident_b, ps_w, ps_b2, ps_y,
                   xc_h, z_h, ybf_h, wdt_b, dtb_t, dp_t, dt_t, u_t, xcf, zf, y2, yf,
                   ybt, dA, add_t, h_t, tmp_t, spz, spt, spr)
    p.barrier()
    st2.close()


def _c_scan_ec(nc, p, mW, dr, fwd, ec, A_sb, sel_b, ident_b, ps_w, ps_b2, ps_y,
               xc_h, z_h, ybf_h, wdt_b, dtb_t, dp_t, dt_t, u_t, xcf, zf, y2, yf,
               ybt, dA, add_t, h_t, tmp_t, spz, spt, spr):
    acb = (dr * ECH + ec) * NST
    p.dma("sp", dtb_t[:], mW["dtb"][dr][ec * 128:(ec + 1) * 128, :], writes=["dtb"])
    p.dma("sp", dp_t[:], mW["Dp"][dr][ec * 128:(ec + 1) * 128, :], writes=["dp"])
    p.dma("sp", xcf[:], xc_h[ec * 128:(ec + 1) * 128, :], reads=[("xc_h", ec)], writes=["c_xcf"])
    p.dma("sp", zf[:], z_h[ec * 128:(ec + 1) * 128, :], reads=[("z_h", ec)], writes=["c_zf"])
    for tc in range(4):
        p.op("pe", lambda e, tc=tc: e.matmul(
            ps_w[:, (tc % 2) * 512:((tc % 2) + 1) * 512],
            wdt_b[0:64, ec * 128:(ec + 1) * 128],
            _DLB[0:64, tc * 512:(tc + 1) * 512], start=True, stop=True,
            skip_group_check=True),
            reads=["wdtb", "c_dlB"], writes=["ps_cwALL"])
        p.op("act", lambda e, tc=tc: e.activation(
            spz[:], ps_w[:, (tc % 2) * 512:((tc % 2) + 1) * 512], AF.Identity, bias=dtb_t[:]),
            reads=["ps_cwALL"], hard=["dtb"], writes=["c_spz"])
        p.op("act", lambda e: e.activation(spt[:], spz[:], AF.Abs),
            reads=["c_spz"], writes=["c_spt"])
        p.op("act", lambda e: e.activation(spt[:], spt[:], AF.Exp, scale=-1.0),
            reads=["c_spt"], writes=["c_spt"])
        p.op("act", lambda e: e.activation(spt[:], spt[:], AF.Ln, bias=1.0),
            reads=["c_spt"], writes=["c_spt"])
        p.op("ve", lambda e: e.tensor_scalar_max(spr[:], spz[:], 0.0),
            reads=["c_spz"], writes=["c_spr"])
        p.op("ve", lambda e, tc=tc: e.tensor_add(dt_t[:, tc * 512:(tc + 1) * 512], spr[:], spt[:]),
            reads=["c_spr", "c_spt"], writes=["c_dt"])
    p.op("ve", lambda e: e.tensor_mul(u_t[:], dt_t[:], xcf[:]),
         reads=["c_dt", "c_xcf"], writes=["c_u"])
    for n in range(NST):
        sl = n % 2
        p.op("act", lambda e, sl=sl, n=n: e.activation(
            dA[sl][:], dt_t[:], AF.Exp, scale=A_sb[:, acb + n:acb + n + 1]),
            reads=["c_dt"], hard=[("A", acb)], writes=[("dA", sl)])
        for th in range(2):
            pb = ps_b2 if th == 0 else ps_w
            pbk = "ps_cb" if th == 0 else "ps_cwALL"
            for q in range(2):
                p.op("pe", lambda e, pb=pb, n=n, th=th, q=q: e.matmul(
                    pb[:, q * 512:(q + 1) * 512], sel_b[:, n * 128:(n + 1) * 128],
                    _DLB[:, th * 1024 + q * 512:th * 1024 + (q + 1) * 512],
                    start=True, stop=True, skip_group_check=True),
                    reads=["sel_b", "c_dlB"], writes=[pbk])
            p.op("ve", lambda e, pb=pb, th=th: e.tensor_mul(
                add_t[:, th * 1024:(th + 1) * 1024], u_t[:, th * 1024:(th + 1) * 1024],
                pb[:, 0:1024]),
                reads=["c_u", pbk], writes=["c_add"])
        if fwd:
            p.op("ve", lambda e, sl=sl: e.tensor_tensor_scan(
                h_t[:], dA[sl][:], add_t[:], 0.0, ALU.mult, ALU.add),
                reads=[("dA", sl), "c_add"], writes=["c_h"])
        else:
            p.op("ve", lambda e, sl=sl: e.tensor_tensor_scan(
                h_t[:, ::-1], dA[sl][:, ::-1], add_t[:, ::-1], 0.0, ALU.mult, ALU.add),
                reads=[("dA", sl), "c_add"], writes=["c_h"])
        for th in range(2):
            pb = ps_b2 if th == 0 else ps_w
            pbk = "ps_cb" if th == 0 else "ps_cwALL"
            for q in range(2):
                p.op("pe", lambda e, pb=pb, n=n, th=th, q=q: e.matmul(
                    pb[:, q * 512:(q + 1) * 512], sel_b[:, n * 128:(n + 1) * 128],
                    _CC[:, th * 1024 + q * 512:th * 1024 + (q + 1) * 512],
                    start=True, stop=True, skip_group_check=True),
                    reads=["sel_b", "c_C"], writes=[pbk])
            p.op("ve", lambda e, pb=pb, th=th, sl=sl: e.tensor_mul(
                tmp_t[sl][:, th * 1024:(th + 1) * 1024],
                h_t[:, th * 1024:(th + 1) * 1024], pb[:, 0:1024]),
                reads=["c_h", pbk], writes=[("tmp", sl, th)])
        for tc in range(4):
            p.op("pe", lambda e, tc=tc, n=n, sl=sl: e.matmul(
                ps_y[:, tc * 512:(tc + 1) * 512], ident_b[:],
                tmp_t[sl][:, tc * 512:(tc + 1) * 512],
                start=(n == 0), stop=(n == NST - 1), skip_group_check=True),
                reads=[("tmp", sl, tc // 2), "ident_b"], writes=[("ps_cy", tc)])
    p.op("act", lambda e: e.activation(yf[:], ps_y[:, :], AF.Copy),
         reads=[("ps_cy", 0), ("ps_cy", 1), ("ps_cy", 2), ("ps_cy", 3)], writes=["c_yfv"])
    p.op("ve", lambda e: e.scalar_tensor_tensor(y2[:], xcf[:], dp_t[:], yf[:], ALU.mult, ALU.add),
         reads=["c_xcf", "c_yfv"], hard=["dp"], writes=["c_y2"])
    p.op("act", lambda e: e.activation(zf[:], zf[:], AF.Silu), reads=["c_zf"], writes=["c_zf"])
    p.op("ve", lambda e: e.tensor_mul(ybt[:], y2[:], zf[:]), reads=["c_y2", "c_zf"], writes=["c_ybt"])
    p.dma("sp", ybf_h[ec * 128:(ec + 1) * 128, :], ybt[:], reads=["c_ybt"], writes=[("ybf_h", ec)])


def _c_stage3(nc, p, mW, dr, ps_w, ps_b2, ps_y, ybf_h, xm_h):
    st3 = ExitStack()
    ytc = _sbu(nc, st3, "c_ytc", [128, ECH * 512], BF16)
    wo_f = _sbu(nc, st3, "c_wof", [128, D], F32)
    wo_b = _sbu(nc, st3, "c_wob", [128, D], BF16)
    xms = _sbu(nc, st3, "c_xms", [128, 512], F32)
    xmr = _sbu(nc, st3, "c_xmr", [128, 512], F32)
    banks = [(ps_b2, 0), (ps_b2, 512), (ps_w, 0), (ps_w, 512),
             (ps_y, 0), (ps_y, 512), (ps_y, 1024), (ps_y, 1536)]
    p.barrier()
    for tc in range(4):
        for ec in range(ECH):
            p.dma("sp", ytc[:, ec * 512:(ec + 1) * 512],
                  ybf_h[ec * 128:(ec + 1) * 128, tc * 512:(tc + 1) * 512],
                  reads=[("ybf_h", ec)], writes=[("ytc", ec)])
        for ec in range(ECH):
            p.dma("gp", wo_f[:], mW["WoutT"][dr][ec * 128:(ec + 1) * 128, :], writes=["wof"])
            p.op("act", lambda e: e.activation(wo_b[:], wo_f[:], AF.Copy),
                 reads=["wof"], writes=["wob"])
            for dc in range(DCH):
                pb, off = banks[dc]
                p.op("pe", lambda e, pb=pb, off=off, dc=dc, ec=ec: e.matmul(
                    pb[:, off:off + 512], wo_b[:, dc * 128:(dc + 1) * 128],
                    ytc[:, ec * 512:(ec + 1) * 512],
                    start=(ec == 0), stop=(ec == ECH - 1), skip_group_check=True),
                    reads=["wob", ("ytc", ec)], writes=[("k7", dc)])
        for dc in range(DCH):
            pb, off = banks[dc]
            if dr == 0:
                p.op("act", lambda e, pb=pb, off=off: e.activation(
                    xms[:], pb[:, off:off + 512], AF.Copy),
                    reads=[("k7", dc)], writes=["c_xms"])
            else:
                p.dma("sp", xmr[:], xm_h[dc * 128:(dc + 1) * 128, tc * 512:(tc + 1) * 512],
                      reads=[("xm_h", dc, tc)], writes=["c_xmr"])
                p.op("ve", lambda e, pb=pb, off=off: e.tensor_add(
                    xms[:], xmr[:], pb[:, off:off + 512]),
                    reads=["c_xmr", ("k7", dc)], writes=["c_xms"])
            p.dma("sp", xm_h[dc * 128:(dc + 1) * 128, tc * 512:(tc + 1) * 512], xms[:],
                  reads=["c_xms"], writes=[("xm_h", dc, tc)])
    p.barrier()
    st3.close()


def _c_stage4(nc, p, b, bc, ident_f, ps_y, xm_h, xn_h, x2_h, x3_h, xn3T_h):
    st4 = ExitStack()
    tinyC = alloc_tiny(nc, st4, ["c0", "c1", "c2"], p)
    xmF = _sbu(nc, st4, "c8_xmF", [128, D], F32)
    xmt = _sbu(nc, st4, "c8_xmt", [128, D], F32)
    scrC = _sbu(nc, st4, "c8_scr", [128, D], F32)
    xnr = _sbu(nc, st4, "c8_xnr", [128, D], F32)
    retv = _sbu(nc, st4, "c8_ret", [128, D], F32)
    x2r = _sbu(nc, st4, "c8_x2r", [128, D], F32)
    x3t = _sbu(nc, st4, "c8_x3t", [128, D], F32)
    xn3 = _sbu(nc, st4, "c8_xn3", [128, D], F32)
    xn3F = _sbu(nc, st4, "c8_xn3F", [128, D], F32)
    for i in range(NTB):
        it = b * NTB + i
        tt = i * 128
        for dc in range(DCH):
            p.dma("sp", xmF[:, dc * 128:(dc + 1) * 128],
                  xm_h[dc * 128:(dc + 1) * 128, tt:tt + 128],
                  reads=[("xm_h", dc, tt // 512)], writes=[("xmF", dc)])
        for dc in range(DCH):
            p.op("pe", lambda e, dc=dc: e.transpose(
                ps_y[:, dc * 128:(dc + 1) * 128], xmF[:, dc * 128:(dc + 1) * 128], ident_f[:]),
                reads=[("xmF", dc), "ident_f"], writes=[("ps_cy", 0), ("ps_cy", 1)])
        p.op("act", lambda e: e.activation(xmt[:], ps_y[:, 0:1024], AF.Copy),
             reads=[("ps_cy", 0), ("ps_cy", 1)], writes=["xmt"])
        r0, n0, h0 = ln_stats(p, xmt[:], "xmt", scrC[:], "scrC", tinyC, "c0", D)
        ln_apply(p, retv[:], "retv", xmt[:], "xmt", r0, n0, h0)
        p.op("ve", lambda e: e.tensor_mul(retv[:], retv[:], bc["ln2w"][:]),
             reads=["retv", "bc_ln2w"], writes=["retv"])
        p.op("ve", lambda e: e.tensor_add(retv[:], retv[:], bc["ln2b"][:]),
             reads=["retv", "bc_ln2b"], writes=["retv"])
        p.dma("sp", xnr[:], xn_h[it * 128:(it + 1) * 128, :], reads=[("xn_h", it)], writes=["xnr"])
        p.op("ve", lambda e: e.tensor_add(retv[:], retv[:], xnr[:]),
             reads=["retv", "xnr"], writes=["retv"])
        r1, n1, h1 = ln_stats(p, retv[:], "retv", scrC[:], "scrC", tinyC, "c1", D)
        ln_apply(p, xn3[:], "lnretv", retv[:], "retv", r1, n1, h1)
        p.dma("sp", x2r[:], x2_h[it * 128:(it + 1) * 128, :], reads=[("x2_h", it)], writes=["x2r"])
        p.op("ve", lambda e: e.tensor_add(x3t[:], x2r[:], xn3[:]),
             reads=["x2r", "lnretv"], writes=["x3t"])
        p.dma("sp", x3_h[it * 128:(it + 1) * 128, :], x3t[:], reads=["x3t"], writes=[("x3_h", it)])
        r2, n2, h2 = ln_stats(p, x3t[:], "x3t", scrC[:], "scrC", tinyC, "c2", D)
        ln_apply(p, xn3[:], "xn3", x3t[:], "x3t", r2, n2, h2)
        for dc in range(DCH):
            p.op("pe", lambda e, dc=dc: e.transpose(
                ps_y[:, 1024 + dc * 128:1024 + (dc + 1) * 128],
                xn3[:, dc * 128:(dc + 1) * 128], ident_f[:]),
                reads=["xn3", "ident_f"], writes=[("ps_cy", 2), ("ps_cy", 3)])
        p.op("act", lambda e: e.activation(xn3F[:], ps_y[:, 1024:2048], AF.Copy),
             reads=[("ps_cy", 2), ("ps_cy", 3)], writes=["xn3F"])
        for dc in range(DCH):
            p.dma("sp", xn3T_h[b, dc * 128:(dc + 1) * 128, tt:tt + 128],
                  xn3F[:, dc * 128:(dc + 1) * 128],
                  reads=["xn3F"], writes=[("xn3T_h", b, dc, i)])
    p.barrier()
    st4.close()


def build_phase_d(nc, p, stack, WrT, WkkT, WvvT, murc, mukc,
                  xn3T_h, x3_h, z_h, xm_h, out_hbm, ident_f):
    stD = ExitStack()
    ps_a = stD.enter_context(nc.psum_tensor("ps_da", [128, 2048], F32))
    ps_b = stD.enter_context(nc.psum_tensor("ps_db", [128, 2048], F32))
    mur_t = _sbu(nc, stD, "d_mur", [128, DCH], F32)
    muk_t = _sbu(nc, stD, "d_muk", [128, DCH], F32)
    p.dma("sp", mur_t[:], murc.rearrange("(c p) one -> p (c one)", p=128), writes=["d_mur"], slow=True)
    p.dma("sp", muk_t[:], mukc.rearrange("(c p) one -> p (c one)", p=128), writes=["d_muk"], slow=True)
    for b in range(BL):
        _d_body(nc, p, b, WrT, WkkT, WvvT, mur_t, muk_t, xn3T_h, x3_h, z_h, xm_h,
                out_hbm, ident_f, ps_a, ps_b)
    stD.close()


def _d_body(nc, p, b, WrT, WkkT, WvvT, mur_t, muk_t, xn3T_h, x3_h, z_h, xm_h,
            out_hbm, ident_f, ps_a, ps_b):
    banksD = [(ps_a, 0), (ps_a, 512), (ps_a, 1024), (ps_a, 1536),
              (ps_b, 0), (ps_b, 512), (ps_b, 1024), (ps_b, 1536)]
    st1 = ExitStack()
    xsh = _sbu(nc, st1, "d_xsh", [128, L + 1], F32)
    dbuf = _sbu(nc, st1, "d_dbuf", [128, L], F32)
    xr_sb = _sbu(nc, st1, "d_xr", [128, DCH * L], BF16)
    xk_sb = _sbu(nc, st1, "d_xk", [128, DCH * L], BF16)
    ka_sb = _sbu(nc, st1, "d_ka", [128, DCH * L], BF16)
    wd_f = _sbu(nc, st1, "d_wf", [128, D], F32)
    wd_b = _sbu(nc, st1, "d_wb", [128, D], BF16)
    ev1 = _sbu(nc, st1, "d_ev1", [128, 512], F32)
    ev2 = _sbu(nc, st1, "d_ev2", [128, 512], F32)
    for dc in range(DCH):
        p.op("gp", lambda e: e.memset(xsh[:, 0:1], 0.0), reads=["d_xsh"], writes=["d_xsh"])
        p.dma("sp", xsh[:, 1:L + 1], xn3T_h[b, dc * 128:(dc + 1) * 128, :],
              reads=[("xn3T_h", b, dc, i2) for i2 in range(NTB)], writes=["d_xsh"])
        p.op("ve", lambda e: e.tensor_sub(dbuf[:], xsh[:, 1:L + 1], xsh[:, 0:L]),
             reads=["d_xsh"], writes=["d_dbuf"])
        p.op("ve", lambda e, dc=dc: e.scalar_tensor_tensor(
            xr_sb[:, dc * L:(dc + 1) * L], dbuf[:], mur_t[:, dc:dc + 1], xsh[:, 0:L],
            ALU.mult, ALU.add),
            reads=["d_dbuf", "d_xsh"], hard=["d_mur"], writes=[("xr", dc)])
        p.op("ve", lambda e, dc=dc: e.scalar_tensor_tensor(
            xk_sb[:, dc * L:(dc + 1) * L], dbuf[:], muk_t[:, dc:dc + 1], xsh[:, 0:L],
            ALU.mult, ALU.add),
            reads=["d_dbuf", "d_xsh"], hard=["d_muk"], writes=[("xk", dc)])
    for tc in range(4):
        for kc in range(DCH):
            p.dma("gp", wd_f[:], WkkT[kc * 128:(kc + 1) * 128, :], writes=["d_wf"])
            p.op("act", lambda e: e.activation(wd_b[:], wd_f[:], AF.Copy),
                 reads=["d_wf"], writes=["d_wb"])
            for dc in range(DCH):
                pb, off = banksD[dc]
                p.op("pe", lambda e, pb=pb, off=off, dc=dc, kc=kc, tc=tc: e.matmul(
                    pb[:, off:off + 512], wd_b[:, dc * 128:(dc + 1) * 128],
                    xk_sb[:, kc * L + tc * 512:kc * L + (tc + 1) * 512],
                    start=(kc == 0), stop=(kc == DCH - 1), skip_group_check=True),
                    reads=["d_wb", ("xk", kc)], writes=[("dbank", dc)])
        for dc in range(DCH):
            pb, off = banksD[dc]
            p.op("act", lambda e, pb=pb, off=off: e.activation(ev1[:], pb[:, off:off + 512], AF.Relu),
                 reads=[("dbank", dc)], writes=["d_ev1"])
            p.op("act", lambda e, dc=dc, tc=tc: e.activation(
                ka_sb[:, dc * L + tc * 512:dc * L + (tc + 1) * 512], ev1[:], AF.Square),
                reads=["d_ev1"], writes=[("ka", dc)])
    for tc in range(4):
        for kc in range(DCH):
            p.dma("gp", wd_f[:], WvvT[kc * 128:(kc + 1) * 128, :], writes=["d_wf"])
            p.op("act", lambda e: e.activation(wd_b[:], wd_f[:], AF.Copy),
                 reads=["d_wf"], writes=["d_wb"])
            for dc in range(DCH):
                pb, off = banksD[dc]
                p.op("pe", lambda e, pb=pb, off=off, dc=dc, kc=kc, tc=tc: e.matmul(
                    pb[:, off:off + 512], wd_b[:, dc * 128:(dc + 1) * 128],
                    ka_sb[:, kc * L + tc * 512:kc * L + (tc + 1) * 512],
                    start=(kc == 0), stop=(kc == DCH - 1), skip_group_check=True),
                    reads=["d_wb", ("ka", kc)], writes=[("dbank", dc)])
        for dc in range(DCH):
            pb, off = banksD[dc]
            p.op("act", lambda e, pb=pb, off=off: e.activation(ev1[:], pb[:, off:off + 512], AF.Copy),
                 reads=[("dbank", dc)], writes=["d_ev1"])
            p.dma("sp", z_h[dc * 128:(dc + 1) * 128, tc * 512:(tc + 1) * 512], ev1[:],
                  reads=["d_ev1"], writes=[("v2_h", dc, tc)])
    for tc in range(4):
        for kc in range(DCH):
            p.dma("gp", wd_f[:], WrT[kc * 128:(kc + 1) * 128, :], writes=["d_wf"])
            p.op("act", lambda e: e.activation(wd_b[:], wd_f[:], AF.Copy),
                 reads=["d_wf"], writes=["d_wb"])
            for dc in range(DCH):
                pb, off = banksD[dc]
                p.op("pe", lambda e, pb=pb, off=off, dc=dc, kc=kc, tc=tc: e.matmul(
                    pb[:, off:off + 512], wd_b[:, dc * 128:(dc + 1) * 128],
                    xr_sb[:, kc * L + tc * 512:kc * L + (tc + 1) * 512],
                    start=(kc == 0), stop=(kc == DCH - 1), skip_group_check=True),
                    reads=["d_wb", ("xr", kc)], writes=[("dbank", dc)])
        for dc in range(DCH):
            pb, off = banksD[dc]
            p.op("act", lambda e, pb=pb, off=off: e.activation(ev1[:], pb[:, off:off + 512], AF.Sigmoid),
                 reads=[("dbank", dc)], writes=["d_ev1"])
            p.dma("sp", ev2[:], z_h[dc * 128:(dc + 1) * 128, tc * 512:(tc + 1) * 512],
                  reads=[("v2_h", dc, tc)], writes=["d_ev2"])
            p.op("ve", lambda e: e.tensor_mul(ev2[:], ev1[:], ev2[:]),
                 reads=["d_ev1", "d_ev2"], writes=["d_ev2"])
            p.dma("sp", xm_h[dc * 128:(dc + 1) * 128, tc * 512:(tc + 1) * 512], ev2[:],
                  reads=["d_ev2"], writes=[("o_h", dc, tc)])
    p.barrier()
    st1.close()
    st2 = ExitStack()
    tinyD = alloc_tiny(nc, st2, ["d0"], p)
    oF = _sbu(nc, st2, "d_oF", [128, D], F32)
    ot = _sbu(nc, st2, "d_ot", [128, D], F32)
    scrD = _sbu(nc, st2, "d_scr", [128, D], F32)
    lnD = _sbu(nc, st2, "d_ln", [128, D], F32)
    x3r = _sbu(nc, st2, "d_x3r", [128, D], F32)
    x4 = _sbu(nc, st2, "d_x4", [128, D], F32)
    for i in range(NTB):
        it = b * NTB + i
        tt = i * 128
        for dc in range(DCH):
            p.dma("sp", oF[:, dc * 128:(dc + 1) * 128],
                  xm_h[dc * 128:(dc + 1) * 128, tt:tt + 128],
                  reads=[("o_h", dc, tt // 512)], writes=[("oF", dc)])
        for dc in range(DCH):
            p.op("pe", lambda e, dc=dc: e.transpose(
                ps_a[:, dc * 128:(dc + 1) * 128], oF[:, dc * 128:(dc + 1) * 128], ident_f[:]),
                reads=[("oF", dc), "ident_f"], writes=[("dbank", 0), ("dbank", 1)])
        p.op("act", lambda e: e.activation(ot[:], ps_a[:, 0:1024], AF.Copy),
             reads=[("dbank", 0), ("dbank", 1)], writes=["d_ot"])
        r0, n0, h0 = ln_stats(p, ot[:], "d_ot", scrD[:], "d_scr", tinyD, "d0", D)
        ln_apply(p, lnD[:], "d_ln", ot[:], "d_ot", r0, n0, h0)
        p.dma("sp", x3r[:], x3_h[it * 128:(it + 1) * 128, :], reads=[("x3_h", it)], writes=["d_x3r"])
        p.op("ve", lambda e: e.tensor_add(x4[:], x3r[:], lnD[:]),
             reads=["d_x3r", "d_ln"], writes=["d_x4"])
        p.dma("sp", out_hbm[it * 128:(it + 1) * 128, :], x4[:],
              reads=["d_x4"], writes=[("out_h", it)])
    p.barrier()
    st2.close()


def _host_inputs(inputs, core):
    ct = np.ascontiguousarray
    x = inputs["L0_attn_hidden_states"][2 * core:2 * core + 2].reshape(TOK, D)
    m = {"x_in": ct(x)}
    m["Wq"] = ct(inputs["L0_attn_Wq"]); m["WqT"] = ct(inputs["L0_attn_Wq"].T)
    m["Wk"] = ct(inputs["L0_attn_Wk"]); m["Wv"] = ct(inputs["L0_attn_Wv"])
    m["Wqa"] = ct(inputs["L0_attn_Wqa"]); m["Wka"] = ct(inputs["L0_attn_Wka"])
    for nm, key in (("bq", "L0_attn_bq"), ("bk", "L0_attn_bk"), ("bv", "L0_attn_bv"),
                    ("bqa", "L0_attn_bqa"), ("bka", "L0_attn_bka"),
                    ("g0", "L0_attn_gamma"),
                    ("gff", "L0_ffn_gamma"), ("bff", "L0_ffn_beta"),
                    ("ln1w", "L1_attn_ln1_w"), ("ln1b", "L1_attn_ln1_b"),
                    ("ln2w", "L1_attn_ln2_w"), ("ln2b", "L1_attn_ln2_b")):
        m[nm] = ct(inputs[key].reshape(1, D))
    m["be0c"] = ct(inputs["L0_attn_beta"].reshape(D, 1))
    m["wffT"] = ct(inputs["L0_ffn_weight"].T)
    for mi, pre in ((1, "L1_attn_m1_"), (2, "L1_attn_m2_")):
        m[f"m{mi}_WinT"] = ct(inputs[pre + "W_in"].T)
        m[f"m{mi}_convw"] = ct(inputs[pre + "conv_w"])
        m[f"m{mi}_convb"] = ct(inputs[pre + "conv_b"].reshape(E, 1))
        m[f"m{mi}_WxpT"] = ct(inputs[pre + "W_xp"].T)
        m[f"m{mi}_WdtT"] = ct(inputs[pre + "W_dt"].T)
        m[f"m{mi}_dtb"] = ct(inputs[pre + "dt_b"].reshape(E, 1))
        m[f"m{mi}_Alog"] = ct(inputs[pre + "A_log"])
        m[f"m{mi}_Dp"] = ct(inputs[pre + "D"].reshape(E, 1))
        m[f"m{mi}_WoutT"] = ct(inputs[pre + "W_out"].T)
    m["WrT"] = ct(inputs["L1_ffn_W_r"].T)
    m["WkkT"] = ct(inputs["L1_ffn_W_k"].T)
    m["WvvT"] = ct(inputs["L1_ffn_W_v"].T)
    m["murc"] = ct(inputs["L1_ffn_mu_r"].reshape(D, 1))
    m["mukc"] = ct(inputs["L1_ffn_mu_k"].reshape(D, 1))
    return m


def kernel(**inputs):
    key = (_PHASES, _TAP)
    if key not in _NC_CACHE:
        _NC_CACHE[key] = build()
    nc = _NC_CACHE[key]
    in_maps = [_host_inputs(inputs, c) for c in range(NCORES)]
    res = run_bass_kernel_spmd(nc, in_maps, core_ids=list(range(NCORES)))
    return np.concatenate([res.results[c]["out"].reshape(BL, L, D) for c in range(NCORES)], axis=0)
